# version 18
# baseline (speedup 1.0000x reference)
import numpy as np
import ml_dtypes
import concourse.bass as bass
import concourse.mybir as mybir
from concourse.bass_types import DynSlice
from concourse.bass_utils import run_bass_kernel_spmd

F32 = mybir.dt.float32
BF16 = mybir.dt.bfloat16
AF = mybir.ActivationFunctionType
ALU = mybir.AluOpType
AX = mybir.AxisListType
bfdt = ml_dtypes.bfloat16

NCORE = 8
BOUND_SPREAD = True
D = 2048
HID = 8192
TOKC = 1024
NDC = D // 128
NHC = HID // 128
T = 4096
MASKV = -30000.0
SCRATCH = 28672

SEMS = ("pe", "dve", "act", "gp", "cc",
        "sw0", "sw1", "sw2", "sw3", "sx0", "sx1", "sqs", "scs", "sos0", "sos1",
        "sa0", "sa1", "sg0", "sg1", "sd0", "sd1", "sd2", "sd3",
        "gconst", "gstage", "ga0", "ga1", "grope", "gg", "gka", "gvt", "gqa",
        "gmsk", "gm2", "gc0", "gc1", "gcp", "gqa0", "gqa1", "skv", "gstage2",
        "gr0", "gr1", "gr2", "gr3", "gr4", "gr5", "gr6", "gr7", "gr8", "gr9",
        "gr10", "gr11")
ENGS = ("pe", "dve", "act", "gp", "sp")
OWN = {"pe": "pe", "dve": "dve", "act": "act", "gp": "gp", "sp": None}


class Buf:
    __slots__ = ("wr", "rd")

    def __init__(self, init=()):
        self.wr = []
        self.rd = list(init)

    def rdep(self):
        return list(self.wr)

    def wdep(self):
        return list(self.wr) + list(self.rd)

    def w(self, toks):
        if isinstance(toks, tuple):
            toks = [toks]
        self.wr = [t for t in toks if t is not None]
        self.rd = []

    def r(self, tok):
        if tok is not None:
            self.rd.append(tok)


def host_constants():
    c = {}
    c["idenf"] = np.eye(128, dtype=np.float32)
    c["idenb"] = np.eye(128, dtype=bfdt)
    c["onesb"] = np.ones((128, 128), dtype=bfdt)
    indk = np.zeros((128, 4 * 8), dtype=bfdt)
    for kc in range(4):
        indk[0:64, kc * 8 + 2 * kc] = 1
        indk[64:128, kc * 8 + 2 * kc + 1] = 1
    c["indk"] = indk
    rep = np.zeros((8, 128), dtype=bfdt)
    for m in range(128):
        rep[m % 8, m] = 1
    c["repmap"] = rep
    msk = np.zeros((128, 4 * 512), dtype=bfdt)
    kk = np.arange(128)[:, None]
    qq = np.arange(512)[None, :]
    for j in range(4):
        msk[:, j * 512:(j + 1) * 512] = np.where(kk + 128 * j > qq, 0.0, 1.0).astype(bfdt)
    c["masks"] = msk
    return c


def rope_tables():
    ex = (np.arange(0, 64, 2, dtype=np.float32) / np.float32(64.0)).astype(np.float32)
    th = (np.float32(1.0) / np.power(np.float32(10000.0), ex)).astype(np.float32)
    pos = np.arange(T, dtype=np.float32)
    freqs = (pos[:, None] * th[None, :]).astype(np.float32)
    cos = np.cos(freqs.astype(np.float64)).astype(np.float32)
    sin = np.sin(freqs.astype(np.float64)).astype(np.float32)
    cos_t, sin_t = [], []
    for q in range(4):
        cb = cos[q * 1024:(q + 1) * 1024, :].T
        sb = sin[q * 1024:(q + 1) * 1024, :].T
        cos_t.append(np.ascontiguousarray(np.tile(cb, (4, 1))))
        sin_t.append(np.ascontiguousarray(np.concatenate([sb, -sb, sb, -sb], axis=0)))
    return cos_t, sin_t


def build_nc(phases=5, use_ag=None):
    if use_ag is None:
        use_ag = True
    nc = bass.Bass("TRN2", target_bir_lowering=False, num_devices=NCORE,
                   dynamic_dma_scratch_size=SCRATCH,
                   detect_race_conditions=False)

    x_in = nc.dram_tensor("x_slice", [TOKC, D], F32, kind="ExternalInput").ap()
    wq_sh = nc.dram_tensor("wq_sh", [256, 2048], F32, kind="ExternalInput").ap()
    wk_sh = nc.dram_tensor("wk_sh", [256, 512], F32, kind="ExternalInput").ap()
    wv_sh = nc.dram_tensor("wv_sh", [256, 512], F32, kind="ExternalInput").ap()
    wo_sh = nc.dram_tensor("wo_sh", [256, 2048], F32, kind="ExternalInput").ap()
    wg_sh = nc.dram_tensor("wg_sh", [8192, 256], F32, kind="ExternalInput").ap()
    wu_sh = nc.dram_tensor("wu_sh", [8192, 256], F32, kind="ExternalInput").ap()
    wd_sh = nc.dram_tensor("wd_sh", [2048, 1024], F32, kind="ExternalInput").ap()
    ropec = nc.dram_tensor("rope_cos", [128, 1024], F32, kind="ExternalInput").ap()
    ropes = nc.dram_tensor("rope_sin2", [128, 1024], F32, kind="ExternalInput").ap()
    out_d = nc.dram_tensor("out_slice", [TOKC, D], F32, kind="ExternalOutput").ap()

    wg_in = nc.dram_tensor("wg_in", [256, 8192], BF16).ap()
    wu_in = nc.dram_tensor("wu_in", [256, 8192], BF16).ap()
    wdt_in = nc.dram_tensor("wdt_in", [1024, 2048], BF16).ap()
    wqc_in = nc.dram_tensor("wqc_in", [256, 2048], BF16).ap()
    woc_in = nc.dram_tensor("woc_in", [256, 2048], BF16).ap()
    wkc_in = nc.dram_tensor("wkc_in", [256, 512], BF16).ap()
    wvc_in = nc.dram_tensor("wvc_in", [256, 512], BF16).ap()
    ag_wg = nc.dram_tensor("ag_wg", [2048, 8192], BF16, addr_space="Shared").ap()
    ag_wu = nc.dram_tensor("ag_wu", [2048, 8192], BF16, addr_space="Shared").ap()
    ag_wd = nc.dram_tensor("ag_wd", [8192, 2048], BF16, addr_space="Shared").ap()
    ag_wq = nc.dram_tensor("ag_wq", [2048, 2048], BF16, addr_space="Shared").ap()
    ag_wo = nc.dram_tensor("ag_wo", [2048, 2048], BF16, addr_space="Shared").ap()
    ag_wk = nc.dram_tensor("ag_wk", [2048, 512], BF16, addr_space="Shared").ap()
    ag_wv = nc.dram_tensor("ag_wv", [2048, 512], BF16, addr_space="Shared").ap()
    kv_in = nc.dram_tensor("kv_in", [1024, 1024], BF16).ap()
    kv_all = nc.dram_tensor("kv_all", [8 * 1024, 1024], BF16, addr_space="Shared").ap()
    q_in = nc.dram_tensor("q_in", [2048, 1024], BF16).ap()
    q_all = nc.dram_tensor("q_all", [8 * 2048, 1024], BF16, addr_space="Shared").ap()
    ar_in = nc.dram_tensor("ar_in", [1, 8], F32).ap()
    ar_out = nc.dram_tensor("ar_out", [1, 8], F32).ap()
    ctx_in = nc.dram_tensor("ctx_in", [8192, 256], BF16).ap()
    ctx_all = nc.dram_tensor("ctx_all", [8 * 8192, 256], BF16, addr_space="Shared").ap()
    myq = nc.dram_tensor("myq", [8 * 256, 1024], BF16).ap()
    mykv = nc.dram_tensor("mykv", [8 * 128, 1024], BF16).ap()
    myctx = nc.dram_tensor("myctx", [8 * 1024, 256], BF16).ap()

    hc_ = host_constants()
    idenf_d = nc.inline_tensor(hc_["idenf"], name="idenf_c").ap()
    idenb_d = nc.inline_tensor(hc_["idenb"].astype(np.float32), name="idenb_c").ap()
    onesb_d = nc.inline_tensor(hc_["onesb"].astype(np.float32), name="onesb_c").ap()
    indk_d = nc.inline_tensor(hc_["indk"].astype(np.float32), name="indk_c").ap()
    rep_d = nc.inline_tensor(hc_["repmap"].astype(np.float32), name="repmap_c").ap()
    mask_d = nc.inline_tensor(hc_["masks"].astype(np.float32), name="masks_c").ap()

    def sb(name, shape, dtype, off):
        return nc.alloc_sbuf_tensor_at(name, shape, dtype, offset=off)

    off = SCRATCH + 2048

    def sbn(name, shape, dtype):
        nonlocal off
        t = sb(name, shape, dtype, off)
        nbytes = int(np.prod(shape[1:])) * (2 if dtype == BF16 else 4)
        off += (nbytes + 63) & ~63
        return t

    x_sb = sbn("x_sb", [128, 16 * 1024], F32)
    wbuf = sbn("wbuf", [128, 16384], BF16)
    idenf = sbn("idenf", [128, 128], F32)
    idenb = sbn("idenb", [128, 128], BF16)
    onesb = sbn("onesb", [128, 128], BF16)
    indk = sbn("indk", [128, 32], BF16)
    repmap = sbn("repmap", [8, 128], BF16)
    eps1 = sbn("eps1", [128, 1], F32)
    zb = sbn("zb", [128, 1], F32)
    onesf1 = sbn("onesf1", [128, 64], F32)
    m2s = sbn("m2s", [128, 8], F32)
    m2g = sbn("m2g", [128, 1], F32)
    ARENA = off
    SBUF_END = 229376 - 32
    assert ARENA <= 139100, ARENA

    class Arena:
        def __init__(self, tag, start=None):
            self.off = ARENA if start is None else start
            self.tag = tag

        def al(self, name, shape, dtype):
            t = sb(f"{name}_{self.tag}", shape, dtype, self.off)
            nbytes = int(np.prod(shape[1:])) * (2 if dtype == BF16 else 4)
            self.off += (nbytes + 63) & ~63
            assert self.off <= SBUF_END, (name, self.tag, self.off)
            return t

    pbk = [nc.alloc_psum_tensor(f"pb{i}", [128, 512], F32) for i in range(6)]
    tbb = [nc.alloc_psum_tensor(f"tbb{i}", [128, 1024], BF16) for i in range(2)]

    from contextlib import ExitStack
    es = ExitStack()
    sems = {s: es.enter_context(nc.semaphore(f"sem_{s}")) for s in SEMS}
    st = {e: [] for e in ENGS}
    cnt = {s: 0 for s in SEMS}
    mark = {}

    def emit(e, fn, deps=(), inc=None, inc_n=1):
        m = {}
        for tkn in deps:
            if tkn is None:
                continue
            s, v = tkn
            if v and m.get(s, 0) < v:
                m[s] = v
        for s in sorted(m):
            v = m[s]
            if s == OWN[e]:
                continue
            key = (e, s)
            if mark.get(key, 0) >= v:
                continue
            mark[key] = v
            st[e].append(lambda eng, s=s, v=v: eng.wait_ge(sems[s], v))
        if inc is None:
            st[e].append(fn)
            return None
        cnt[inc] += inc_n
        st[e].append(lambda eng, fn=fn, s=inc, n=inc_n: fn(eng).then_inc(sems[s], n))
        return (inc, cnt[inc])

    def snap():
        return [(s, cnt[s]) for s in SEMS if s != "cc"]

    pbuf = [Buf() for _ in range(8)]
    xch = [[Buf() for _ in range(2)] for _ in range(NDC)]
    wq4 = [Buf() for _ in range(4)]

    for dst, src in [(idenf, idenf_d), (idenb, idenb_d), (onesb, onesb_d),
                     (indk, indk_d), (repmap, rep_d)]:
        tkn = emit("gp", lambda g, dst=dst, src=src: g.dma_start(out=dst[:, :], in_=src),
                   inc="gconst", inc_n=16)
    CONSTS_TOK = tkn
    emit("dve", lambda v: v.memset(eps1[:, :], 1e-5))
    emit("dve", lambda v: v.memset(zb[:, :], 0.0))
    MEMSET_TOK = emit("dve", lambda v: v.memset(onesf1[:, :], 1.0), inc="dve")

    par = Arena("pro")
    stage_in2 = [par.al(f"stage_in{i}", [128, 8192], BF16) for i in range(2)]
    wgs0 = par.al("wgs0", [128, 8192], BF16)
    wgs1 = par.al("wgs1", [128, 8192], BF16)
    wds = [par.al(f"wds{i}", [128, 2048], BF16) for i in range(2)]
    xio = [par.al(f"xio{i}", [128, 2048], F32) for i in range(2)]
    b_stage2 = [Buf(), Buf()]
    b_wgs = [Buf(), Buf()]
    b_wds = [Buf() for _ in range(2)]
    xiob = [Buf(), Buf()]

    tp_rot = [0]
    tbbuf = [Buf(), Buf()]

    def transp(in_ap_fn, out_sb, c0, c1, deps, fp32=False, rows=128):
        slot = tp_rot[0] % 2
        tp_rot[0] += 1
        iden = idenf if fp32 else idenb
        pt = pbk[4 + slot] if fp32 else tbb[slot]
        pbf = pbuf[4 + slot] if fp32 else tbbuf[slot]
        tkn = emit("pe", lambda tt, f=in_ap_fn, iden=iden, rows=rows, pt=pt: tt.transpose(
            pt[0:128, 0:rows], f(), iden[0:rows, 0:rows]),
            deps=list(deps) + pbf.wdep() + [CONSTS_TOK], inc="pe")
        pbf.w(tkn)
        tkn2 = emit("dve", lambda v, o=out_sb, c0=c0, c1=c1, pt=pt: v.tensor_copy(
            o[:, c0:c1], pt[0:128, 0:c1 - c0]), deps=pbf.rdep(), inc="dve")
        pbf.r(tkn2)
        return tkn2

    SKIP_PRO = phases <= -2
    SKIP_AG = phases <= -1
    ag1_store_toks = []
    nast = 0
    SKIP_PRO = phases <= -2
    SKIP_AG = phases <= -1
    for src, dst, r0, w in ([] if SKIP_PRO else [(wq_sh, wqc_in, 0, 2048), (wo_sh, woc_in, 0, 2048),
                            (wk_sh, wkc_in, 0, 512), (wv_sh, wvc_in, 0, 512)]):
        for half in range(2):
            r = nast % 2
            nast += 1
            tkn = emit("gp", lambda g, src=src, half=half, r=r, w=w: g.dma_start(
                out=stage_in2[r][:, 0:w], in_=src[half * 128:(half + 1) * 128, :]),
                deps=b_stage2[r].wdep(), inc=f"ga{r}", inc_n=16)
            b_stage2[r].w(tkn)
            tkn2 = emit("sp", lambda s, dst=dst, r0=r0, half=half, r=r, w=w: s.dma_start(
                out=dst[r0 + half * 128:r0 + (half + 1) * 128, :], in_=stage_in2[r][:, 0:w]),
                deps=b_stage2[r].rdep(), inc=f"sa{r}", inc_n=16)
            b_stage2[r].r(tkn2)
            ag1_store_toks.append(tkn2)

    def ag(in_t, out_t, deps=()):
        return emit("gp", lambda g, in_t=in_t, out_t=out_t: g.collective_compute(
            "AllGather", ALU.bypass, replica_groups=[list(range(NCORE))],
            ins=[in_t], outs=[out_t]), deps=deps, inc="cc")

    gu_ag_toks = {}
    for src, gu_dst in ([] if SKIP_PRO else [(wg_sh, wg_in), (wu_sh, wu_in)]):
        w_toks = []
        for gidx in range(8):
            sg = gidx % 2
            stg = stage_in2[sg]
            tkn = emit("gp", lambda g, src=src, gi=gidx, stg=stg: g.dma_start(
                out=stg[:, 0:2048].rearrange("p (j c) -> p j c", c=256),
                in_=src[gi * 1024:(gi + 1) * 1024, :].rearrange("(j p) c -> p j c", p=128)),
                deps=b_stage2[sg].wdep(), inc=("gstage" if sg == 0 else "gstage2"), inc_n=16)
            b_stage2[sg].w(tkn)
            for j in range(8):
                rg = gidx * 8 + j
                for dh in range(2):
                    tk = transp(lambda j=j, dh=dh, stg=stg: stg[:, j * 256 + dh * 128: j * 256 + (dh + 1) * 128],
                                wgs0 if dh == 0 else wgs1, rg * 128, (rg + 1) * 128,
                                b_stage2[sg].rdep() + b_wgs[dh].wdep())
                    b_stage2[sg].r(tk)
                    b_wgs[dh].r(tk)
        for dh in range(2):
            b_wgs[dh].w(list(b_wgs[dh].rd))
            tkn2 = emit("sp", lambda s, gu_dst=gu_dst, dh=dh: s.dma_start(
                out=gu_dst[dh * 128:(dh + 1) * 128, :],
                in_=(wgs0 if dh == 0 else wgs1)[:, :]),
                deps=b_wgs[dh].rdep(), inc=f"sg{dh}", inc_n=16)
            b_wgs[dh].r(tkn2)
            w_toks.append(tkn2)
        gu_ag_toks[gu_dst.tensor.name] = list(w_toks)
    if not SKIP_AG and not SKIP_PRO:
        CC_WG = ag(wg_in, ag_wg, gu_ag_toks.get("wg_in", []))
        CC_WU = ag(wu_in, ag_wu, gu_ag_toks.get("wu_in", []))

    wd_toks = []
    for hhalf in ([] if SKIP_PRO else range(4)):
        for gidx in range(2):
            sg = gidx % 2
            stg = stage_in2[sg]
            tkn = emit("gp", lambda g, gi=gidx, stg=stg: g.dma_start(
                out=stg[:, :].rearrange("p (j c) -> p j c", c=1024),
                in_=wd_sh[gi * 1024:(gi + 1) * 1024, :].rearrange("(j p) c -> p j c", p=128)),
                deps=b_stage2[sg].wdep(), inc=("gstage" if sg == 0 else "gstage2"), inc_n=16)
            b_stage2[sg].w(tkn)
            for j in range(8):
                dc = gidx * 8 + j
                for hl in range(2):
                    hcl = hhalf * 2 + hl
                    tk = transp(lambda j=j, hcl=hcl, stg=stg: stg[:, j * 1024 + hcl * 128: j * 1024 + (hcl + 1) * 128],
                                wds[hl], dc * 128, (dc + 1) * 128,
                                b_stage2[sg].rdep() + b_wds[hl].wdep())
                    b_stage2[sg].r(tk)
                    b_wds[hl].r(tk)
        for hl in range(2):
            hcl = hhalf * 2 + hl
            b_wds[hl].w(list(b_wds[hl].rd))
            tkn2 = emit("sp", lambda s, hcl=hcl, hl=hl: s.dma_start(
                out=wdt_in[hcl * 128:(hcl + 1) * 128, :], in_=wds[hl][:, :]),
                deps=b_wds[hl].rdep(), inc=f"sd{hl}", inc_n=16)
            b_wds[hl].r(tkn2)
            wd_toks.append(tkn2)

    if not SKIP_AG:
        if SKIP_PRO:
            CC_WG = ag(wg_in, ag_wg, [])
            CC_WU = ag(wu_in, ag_wu, [])
        CC_WD = ag(wdt_in, ag_wd, wd_toks)
        CC_WQ = ag(wqc_in, ag_wq, ag1_store_toks)
        CC_WK = ag(wkc_in, ag_wk)
        CC_WV = ag(wvc_in, ag_wv)
        CC_WO = ag(woc_in, ag_wo)

    for tt in range(8):
        r = tt % 2
        tloc = tt // 4
        tkn = emit("sp", lambda s, tt=tt, r=r: s.dma_start(
            out=xio[r][:, :], in_=x_in[tt * 128:(tt + 1) * 128, :]),
            deps=xiob[r].wdep(), inc=f"sx{r}", inc_n=16)
        xiob[r].w(tkn)
        for dc in range(NDC):
            tk = transp(lambda dc=dc, r=r: xio[r][:, dc * 128:(dc + 1) * 128],
                        x_sb, dc * 1024 + tt * 128, dc * 1024 + (tt + 1) * 128,
                        xiob[r].rdep() + xch[dc][tloc].wdep(), fp32=True)
            xiob[r].r(tk)
            xch[dc][tloc].r(tk)
    for dc in range(NDC):
        for tl in range(2):
            xch[dc][tl].w(list(xch[dc][tl].rd))

    def emit_ln_stats(t, scr):
        xb2, xq2, mean, var, rstd = (scr["xb2"], scr["xq2"], scr["mean"],
                                     scr["var"], scr["rstd"])
        bxb, bxq = scr["bxb"], scr["bxq"]
        bmean, bvar, brstd = scr["bmean"], scr["bvar"], scr["brstd"]
        for dc in range(NDC):
            r = dc % 2
            xs = x_sb[:, dc * 1024 + t * 512: dc * 1024 + t * 512 + 512]
            tkc = emit("act", lambda a, xs=xs, r=r: a.copy(xb2[:, r * 512:(r + 1) * 512], xs),
                       deps=xch[dc][t].rdep() + bxb[r].wdep(), inc="act")
            bxb[r].w(tkc)
            tkn = emit("dve", lambda v, xs=xs, r=r: v.tensor_mul(
                xq2[:, r * 512:(r + 1) * 512], xs, xs),
                deps=xch[dc][t].rdep() + bxq[r].wdep(), inc="dve")
            bxq[r].w(tkn)
            xch[dc][t].r(tkn)
            xch[dc][t].r(tkc)
            emit("pe", lambda tt_, r=r, dc=dc: tt_.matmul(
                pbk[4][:, :], onesb[:, :], xb2[:, r * 512:(r + 1) * 512],
                start=(dc == 0), stop=(dc == NDC - 1)),
                deps=bxb[r].rdep() + bxq[r].rdep() +
                (pbuf[4].wdep() + pbuf[5].wdep() if dc == 0 else []) +
                [MEMSET_TOK, CONSTS_TOK])
            t2 = emit("pe", lambda tt_, r=r, dc=dc: tt_.matmul(
                pbk[5][:, :], onesb[:, :], xq2[:, r * 512:(r + 1) * 512],
                start=(dc == 0), stop=(dc == NDC - 1)), inc="pe")
            bxb[r].r(t2)
            bxq[r].r(t2)
            if dc == NDC - 1:
                pbuf[4].w(t2)
                pbuf[5].w(t2)
        tkn = emit("dve", lambda v: v.tensor_scalar_mul(mean[:, :], pbk[4][:, :], 1.0 / 2048.0),
                   deps=pbuf[4].rdep() + bmean.wdep(), inc="dve")
        pbuf[4].r(tkn)
        bmean.w(tkn)
        emit("dve", lambda v: v.tensor_mul(var[:, :], mean[:, :], mean[:, :]),
             deps=bmean.rdep() + bvar.wdep())
        tkn = emit("dve", lambda v: v.scalar_tensor_tensor(
            var[:, :], pbk[5][:, :], 1.0 / 2048.0, var[:, :], ALU.mult, ALU.subtract),
            deps=pbuf[5].rdep(), inc="dve")
        pbuf[5].r(tkn)
        bvar.w(tkn)
        tkn = emit("act", lambda a: a.activation(rstd[:, :], var[:, :], AF.Sqrt, bias=eps1[:, :]),
                   deps=bvar.rdep() + brstd.wdep() + [MEMSET_TOK], inc="act")
        bvar.r(tkn)
        tkn = emit("dve", lambda v: v.reciprocal(rstd[:, :], rstd[:, :]), deps=[tkn], inc="dve")
        brstd.w(tkn)

    def emit_ln_normalize(t, scr):
        lnt = scr["ln"]
        lnch = scr["lnch"]
        mean, rstd, xm = scr["mean"], scr["rstd"], scr["xm"]
        bmean, bvar, brstd, bxm = scr["bmean"], scr["bvar"], scr["brstd"], scr["bxm"]
        for dc in range(NDC):
            xs = x_sb[:, dc * 1024 + t * 512: dc * 1024 + t * 512 + 512]
            emit("dve", lambda v, xs=xs: v.tensor_sub(xm[:, :], xs, mean[:, :]),
                 deps=xch[dc][t].rdep() + bmean.rdep() + bxm.wdep() + bvar.rdep())
            tkn = emit("dve", lambda v, dc=dc, lnt=lnt: v.tensor_mul(
                lnt[:, dc * 512:(dc + 1) * 512], xm[:, :], rstd[:, :]),
                deps=lnch[dc].wdep(), inc="dve")
            bxm.w(tkn)
            xch[dc][t].r(tkn)
            bmean.r(tkn)
            brstd.r(tkn)
            lnch[dc].w(tkn)

    def emit_layernorm(t, scr):
        emit_ln_stats(t, scr)
        emit_ln_normalize(t, scr)

    def load_gu_slice(which, s, quarter):
        def fn(sy):
            w = ag_wg if which == 0 else ag_wu
            src = w.rearrange("(dc p) c -> p dc c", p=128)[:, :, s * 256:(s + 1) * 256]
            dst = wbuf[:, quarter * 4096:(quarter + 1) * 4096].rearrange(
                "p (dc c) -> p dc c", c=256)
            return sy.dma_start(out=dst, in_=src)
        return fn

    def load_down_slice(dc, base_q):
        def fn(sy):
            src = ag_wd.rearrange("(hc p) c -> p hc c", p=128)[
                :, :, dc * 128:(dc + 1) * 128]
            dst = wbuf[:, base_q * 4096:(base_q + 2) * 4096].rearrange(
                "p (hc c) -> p hc c", c=128)
            return sy.dma_start(out=dst, in_=src)
        return fn

    def load_proj_slice(kind, s, base_q):
        def fn(sy):
            w = {"q": ag_wq, "o": ag_wo, "k": ag_wk, "v": ag_wv}[kind]
            src = w.rearrange("(dc p) c -> p dc c", p=128)[
                :, :, s * 512:(s + 1) * 512]
            dst = wbuf[:, base_q * 4096:(base_q + 2) * 4096].rearrange(
                "p (dc c) -> p dc c", c=512)
            return sy.dma_start(out=dst, in_=src)
        return fn

    def emit_gate_up(t, scr, gbufs, g_sb):
        lnt = scr["ln"]
        lnch = scr["lnch"]
        for which in (0, 1):
            for s in range(32):
                q = s % 4
                tokw = emit("sp", load_gu_slice(which, s, q),
                            deps=wq4[q].wdep() + [CC_WG if which == 0 else CC_WU],
                            inc=f"sw{q}", inc_n=16)
                wq4[q].w(tokw)
                for hl in range(2):
                    hc = s * 2 + hl
                    bank = hc % 2
                    deps = wq4[q].rdep() + pbuf[bank].wdep()
                    tokm = None
                    for dc in range(NDC):
                        tokm = emit("pe", lambda tt_, dc=dc, hl=hl, q=q, lnt=lnt, bank=bank: tt_.matmul(
                            pbk[bank][:, :],
                            wbuf[:, q * 4096 + dc * 256 + hl * 128: q * 4096 + dc * 256 + (hl + 1) * 128],
                            lnt[:, dc * 512:(dc + 1) * 512],
                            start=(dc == 0), stop=(dc == NDC - 1)),
                            deps=(deps if dc == 0 else []) + lnch[dc].rdep(),
                            inc=("pe" if dc == NDC - 1 else None))
                    pbuf[bank].w(tokm)
                    wq4[q].r(tokm)
                    for dc in range(NDC):
                        lnch[dc].r(tokm)
                    if which == 0:
                        tokg = emit("act", lambda a, hc=hc, bank=bank: a.activation(
                            g_sb[:, hc * 512:(hc + 1) * 512], pbk[bank][:, :],
                            AF.Silu, bias=zb[:, :]),
                            deps=pbuf[bank].rdep() + gbufs[hc].wdep() + [MEMSET_TOK],
                            inc="act")
                    else:
                        tokg = emit("dve", lambda v, hc=hc, bank=bank: v.tensor_mul(
                            g_sb[:, hc * 512:(hc + 1) * 512],
                            g_sb[:, hc * 512:(hc + 1) * 512], pbk[bank][:, :]),
                            deps=pbuf[bank].rdep() + gbufs[hc].wdep(), inc="dve")
                    pbuf[bank].r(tokg)
                    gbufs[hc].w(tokg)

    def emit_down(t, scr, gbufs, g_sb):
        for dc in range(NDC):
            bq = (dc % 2) * 2
            tokd = emit("sp", load_down_slice(dc, bq),
                        deps=wq4[bq].wdep() + wq4[bq + 1].wdep() + [CC_WD],
                        inc=f"sw{bq}", inc_n=16)
            wq4[bq].w(tokd)
            wq4[bq + 1].w(tokd)
            bank = 2 + dc % 2
            deps = [tokd] + pbuf[bank].wdep() + \
                [tk for hcc in range(NHC) for tk in gbufs[hcc].rdep()]
            tokdm = None
            for hcc in range(NHC):
                tokdm = emit("pe", lambda tt_, hcc=hcc, bq=bq, bank=bank: tt_.matmul(
                    pbk[bank][:, :],
                    wbuf[:, bq * 4096 + hcc * 128:bq * 4096 + (hcc + 1) * 128],
                    g_sb[:, hcc * 512:(hcc + 1) * 512],
                    start=(hcc == 0), stop=(hcc == NHC - 1)),
                    deps=(deps if hcc == 0 else []),
                    inc=("pe" if hcc == NHC - 1 else None))
            pbuf[bank].w(tokdm)
            wq4[bq].r(tokdm)
            wq4[bq + 1].r(tokdm)
            for hcc in range(NHC):
                gbufs[hcc].r(tokdm)
            xs = x_sb[:, dc * 1024 + t * 512: dc * 1024 + t * 512 + 512]
            tokev = emit("dve", lambda v, xs=xs, bank=bank: v.scalar_tensor_tensor(
                xs, pbk[bank][:, :], 0.5, xs, ALU.mult, ALU.add),
                deps=pbuf[bank].rdep() + xch[dc][t].wdep(), inc="dve")
            pbuf[bank].r(tokev)
            xch[dc][t].w(tokev)
        return [tokdm, tokev]

    def mk_ln_scr(ar, bar=()):
        vxm = ar.al("vxm", [128, 512], F32)
        return {"ln": ar.al("ln", [128, 16 * 512], BF16),
                "lnch": [Buf(bar) for _ in range(NDC)],
                "xb2": ar.al("xb2", [128, 1024], BF16),
                "xq2": ar.al("xq2", [128, 1024], BF16),
                "mean": ar.al("mean", [128, 512], F32),
                "var": vxm,
                "rstd": ar.al("rstd", [128, 512], F32),
                "xm": vxm,
                "bxb": [Buf(bar), Buf(bar)], "bxq": [Buf(bar), Buf(bar)],
                "bmean": Buf(bar), "bvar": Buf(bar), "brstd": Buf(bar),
                "bxm": Buf(bar)}

    bar = snap()
    far = Arena("f1")
    PH = phases
    fscr = mk_ln_scr(far, bar)
    g_sb = far.al("g", [128, 64 * 512], BF16)
    gbufs = [Buf(bar) for _ in range(NHC)]
    if PH >= 2:
        emit_layernorm(0, fscr)
        emit_gate_up(0, fscr, gbufs, g_sb)
        emit_layernorm(1, fscr)
        emit_down(0, fscr, gbufs, g_sb)
        emit_gate_up(1, fscr, gbufs, g_sb)

    def _emit_output():
        bar2 = snap()
        oar2 = Arena("out")
        xioo2 = [oar2.al(f"xioo{i}", [128, 2048], F32) for i in range(2)]
        xioob2 = [Buf(bar2), Buf(bar2)]
        for tt in range(8):
            r = tt % 2
            tloc = tt // 4
            ev_toks = []
            for dc in range(NDC):
                tk = transp(lambda dc=dc, tt=tt: x_sb[:, dc * 1024 + tt * 128: dc * 1024 + (tt + 1) * 128],
                            xioo2[r], dc * 128, (dc + 1) * 128,
                            xch[dc][tloc].rdep() + xioob2[r].wdep(), fp32=True)
                xch[dc][tloc].r(tk)
                ev_toks.append(tk)
            xioob2[r].w(ev_toks)
            tkn = emit("sp", lambda s, tt=tt, r=r: s.dma_start(
                out=out_d[tt * 128:(tt + 1) * 128, :], in_=xioo2[r][:, :]),
                deps=xioob2[r].rdep(), inc=f"sos{r}", inc_n=16)
            xioob2[r].r(tkn)

    def _finish():
        st["sp"].append(lambda e, v=cnt["sos0"]: e.wait_ge(sems["sos0"], v))
        st["sp"].append(lambda e, v=cnt["sos1"]: e.wait_ge(sems["sos1"], v))
        with nc.Block() as block:
            block.gpsimd(lambda g: [f(g) for f in st["gp"]])
            block.sync(lambda s: [f(s) for f in st["sp"]])
            block.tensor(lambda tt_: [f(tt_) for f in st["pe"]])
            block.vector(lambda v: [f(v) for f in st["dve"]])
            block.scalar(lambda a: [f(a) for f in st["act"]])
        es.close()
        return nc

    if PH < 3:
        if PH >= 2:
            emit_down(1, fscr, gbufs, g_sb)
        _emit_output()
        return _finish()
    qar = Arena("qk")
    qscr = mk_ln_scr(qar, snap())
    emit_layernorm(0, qscr)
    d1_toks = emit_down(1, fscr, gbufs, g_sb)
    bar = snap()
    qkv_own = qar.al("qkv_own", [128, 24 * 1024], BF16)
    bt_s = qar.al("bt", [128, 2048], BF16)
    cosT = qar.al("cosT", [128, 1024], F32)
    sinT = qar.al("sinT", [128, 1024], F32)
    sqs = qar.al("sqs", [128, 512], BF16)
    kn2sb = qar.al("kn2sb", [128, 1024], BF16)
    mxr = qar.al("mxr", [128, 8], F32)
    qkvb = [Buf(bar) for _ in range(24)]
    b_bt = Buf(bar)
    b_sq = Buf(bar)
    b_kn2 = Buf(bar)
    b_mx = Buf(bar)
    b_ropetab = Buf(bar)

    t_c = emit("gp", lambda g: g.dma_start(out=cosT[:, :], in_=ropec),
               deps=b_ropetab.wdep() + d1_toks, inc="ga0", inc_n=16)
    t_s = emit("gp", lambda g: g.dma_start(out=sinT[:, :], in_=ropes), inc="ga1", inc_n=16)
    b_ropetab.w([t_c, t_s])

    kvst_toks = []

    def emit_rope(ch0, th, gi):
        c0 = 0 if th is None else th * 512
        w = 1024 if th is None else 512
        dve_toks = []
        for i in range(2):
            ch = ch0 + i
            emit("dve", lambda v, ch=ch, i=i, c0=c0, w=w: v.tensor_mul(
                bt_s[:, i * 1024 + c0:i * 1024 + c0 + w],
                qkv_own[:, ch * 1024 + c0:ch * 1024 + c0 + w], sinT[:, c0:c0 + w]),
                deps=qkvb[ch].rdep() + b_bt.wdep() + b_ropetab.rdep())
            tkn = emit("dve", lambda v, ch=ch, c0=c0, w=w: v.tensor_mul(
                qkv_own[:, ch * 1024 + c0:ch * 1024 + c0 + w],
                qkv_own[:, ch * 1024 + c0:ch * 1024 + c0 + w], cosT[:, c0:c0 + w]),
                deps=qkvb[ch].wdep(), inc="dve")
            dve_toks.append(tkn)
        b_bt.w(dve_toks)
        b_ropetab.r(dve_toks[-1])
        sw_toks = []
        for d0, s0 in ((0, 32), (32, 0), (64, 96), (96, 64)):
            tkn = emit("gp", lambda g, ch0=ch0, d0=d0, s0=s0, c0=c0, w=w: g.dma_start(
                out=qkv_own[d0:d0 + 32, ch0 * 1024:(ch0 + 2) * 1024].rearrange(
                    "p (j c) -> p j c", c=1024)[:, :, c0:c0 + w],
                in_=bt_s[s0:s0 + 32, 0:2048].rearrange(
                    "p (j c) -> p j c", c=1024)[:, :, c0:c0 + w],
                accum_op=ALU.add),
                deps=dve_toks + b_bt.rdep(), inc=f"gr{gi}", inc_n=16)
            sw_toks.append(tkn)
        b_bt.r(sw_toks[-1])
        for i in range(2):
            qkvb[ch0 + i].w([sw_toks[-1]])

    def emit_kn2(t):
        bank = 4 + t
        for kc in range(4):
            ch = 16 + kc
            tkn = emit("act", lambda a, ch=ch, t=t: a.square(
                sqs[:, :], qkv_own[:, ch * 1024 + t * 512:ch * 1024 + t * 512 + 512]),
                deps=qkvb[ch].rdep() + b_sq.wdep(), inc="act")
            b_sq.w(tkn)
            qkvb[ch].r(tkn)
            tm = emit("pe", lambda tt_, kc=kc, bank=bank: tt_.matmul(
                pbk[bank][0:8, :], indk[:, kc * 8:kc * 8 + 8], sqs[:, :],
                start=(kc == 0), stop=(kc == 3)),
                deps=b_sq.rdep() + (pbuf[bank].wdep() if kc == 0 else []) + [CONSTS_TOK],
                inc="pe")
            if kc == 3:
                pbuf[bank].w(tm)
            b_sq.r(tm)
        tkn = emit("dve", lambda v, t=t, bank=bank: v.tensor_copy(
            kn2sb[0:8, t * 512:(t + 1) * 512], pbk[bank][0:8, :]),
            deps=pbuf[bank].rdep() + b_kn2.wdep(), inc="dve")
        pbuf[bank].r(tkn)
        b_kn2.r(tkn)

    def emit_kv_stores(t, chs, is_v):
        for ch in chs:
            kc = ch - (20 if is_v else 16)
            for half in range(2):
                kvh = 2 * kc + half
                r0 = 128 * kvh + (64 if is_v else 0)
                tkn = emit("sp", lambda s, ch=ch, half=half, r0=r0, t=t: s.dma_start(
                    out=kv_in[r0:r0 + 64, t * 512:t * 512 + 512],
                    in_=qkv_own[64 * half:64 * (half + 1),
                                ch * 1024 + t * 512:ch * 1024 + t * 512 + 512]),
                    deps=qkvb[ch].rdep(), inc="skv", inc_n=16)
                qkvb[ch].r(tkn)
                kvst_toks.append(tkn)

    qst_toks = []

    def emit_q_stores(chs):
        for ch in chs:
            tkn = emit("sp", lambda s, ch=ch: s.dma_start(
                out=q_in[ch * 128:(ch + 1) * 128, :],
                in_=qkv_own[:, ch * 1024:(ch + 1) * 1024]),
                deps=qkvb[ch].rdep(), inc="sqs", inc_n=16)
            qkvb[ch].r(tkn)
            qst_toks.append(tkn)

    CCH = {}

    def emit_m2_ar_ckv():
        b_kn2.w(list(b_kn2.rd))
        for tt2 in range(2):
            bank = 4 + tt2
            tm = emit("pe", lambda tt_, tt2=tt2, bank=bank: tt_.matmul(
                pbk[bank][:, :], repmap[:, :], kn2sb[0:8, tt2 * 512:(tt2 + 1) * 512],
                start=True, stop=True),
                deps=b_kn2.rdep() + pbuf[bank].wdep() + [CONSTS_TOK], inc="pe")
            pbuf[bank].w(tm)
            b_kn2.r(tm)
        emit("dve", lambda v: v.reduce_max(mxr[:, 0:1], pbk[4][:, :], axis=AX.X),
             deps=pbuf[4].rdep() + b_mx.wdep())
        emit("dve", lambda v: v.reduce_max(mxr[:, 1:2], pbk[5][:, :], axis=AX.X),
             deps=pbuf[5].rdep())
        tkn = emit("dve", lambda v: v.tensor_max(mxr[:, 0:1], mxr[:, 0:1], mxr[:, 1:2]), inc="dve")
        pbuf[4].r(tkn)
        pbuf[5].r(tkn)
        b_mx.w(tkn)
        tkn = emit("pe", lambda tt_: tt_.transpose(
            pbk[4][0:1, 0:8], mxr[0:8, 0:1], idenf[0:8, 0:8]),
            deps=b_mx.rdep() + pbuf[4].wdep() + [CONSTS_TOK], inc="pe")
        pbuf[4].w(tkn)
        b_mx.r(tkn)
        tkn = emit("dve", lambda v: v.tensor_copy(mxr[0:1, 0:8], pbk[4][0:1, 0:8]),
                   deps=pbuf[4].rdep() + b_mx.wdep(), inc="dve")
        pbuf[4].r(tkn)
        b_mx.w(tkn)
        tok_ar_in = emit("gp", lambda g: g.dma_start(out=ar_in, in_=mxr[0:1, 0:8]),
                         deps=b_mx.rdep(), inc="gm2", inc_n=16)
        b_mx.r(tok_ar_in)
        CCH["AR"] = emit("gp", lambda g: g.collective_compute(
            "AllReduce", ALU.max, replica_groups=[list(range(NCORE))],
            ins=[ar_in], outs=[ar_out]), deps=[tok_ar_in], inc="cc")
        CCH["KV"] = emit("gp", lambda g: g.collective_compute(
            "AllGather", ALU.bypass, replica_groups=[list(range(NCORE))],
            ins=[kv_in], outs=[kv_all]), deps=list(kvst_toks), inc="cc")

    for t in range(2):
        if t == 1:
            emit_ln_normalize(1, qscr)
        jobs = [("k", 0, 16), ("v", 0, 20)] + [("q", s, 4 * s) for s in range(4)]
        for jn, (kind, s, ch0) in enumerate(jobs):
            if t == 0 and jn == 4:
                emit_ln_stats(1, qscr)
            bq = (jn % 2) * 2
            tokw = emit("sp", load_proj_slice(kind, s, bq),
                        deps=wq4[bq].wdep() + wq4[bq + 1].wdep() +
                        [{"q": CC_WQ, "o": CC_WO, "k": CC_WK, "v": CC_WV}[kind]],
                        inc=f"sw{bq}", inc_n=16)
            wq4[bq].w(tokw)
            wq4[bq + 1].w(tokw)
            for ol in range(4):
                ch = ch0 + ol
                bank = ol % 2
                deps = [tokw] + pbuf[bank].wdep()
                tokm = None
                for dc in range(NDC):
                    tokm = emit("pe", lambda tt_, dc=dc, ol=ol, bq=bq, bank=bank: tt_.matmul(
                        pbk[bank][:, :],
                        wbuf[:, bq * 4096 + dc * 512 + ol * 128: bq * 4096 + dc * 512 + (ol + 1) * 128],
                        qscr["ln"][:, dc * 512:(dc + 1) * 512],
                        start=(dc == 0), stop=(dc == NDC - 1)),
                        deps=(deps if dc == 0 else []) + qscr["lnch"][dc].rdep(),
                        inc=("pe" if dc == NDC - 1 else None))
                pbuf[bank].w(tokm)
                wq4[bq].r(tokm)
                wq4[bq + 1].r(tokm)
                for dc in range(NDC):
                    qscr["lnch"][dc].r(tokm)
                tokev = emit("dve", lambda v, ch=ch, t=t, bank=bank: v.tensor_copy(
                    qkv_own[:, ch * 1024 + t * 512: ch * 1024 + t * 512 + 512],
                    pbk[bank][:, :]),
                    deps=pbuf[bank].rdep() + qkvb[ch].wdep(), inc="dve")
                pbuf[bank].r(tokev)
                qkvb[ch].r(tokev)
            if kind == "k":
                for ch in range(16, 20):
                    qkvb[ch].w(qkvb[ch].wr + list(qkvb[ch].rd))
                emit_rope(16, t, 2 * t)
                emit_rope(18, t, 2 * t + 1)
                emit_kn2(t)
                emit_kv_stores(t, [16, 17, 18, 19], False)
            elif kind == "v":
                for ch in range(20, 24):
                    qkvb[ch].w(qkvb[ch].wr + list(qkvb[ch].rd))
                emit_kv_stores(t, [20, 21, 22, 23], True)
                if t == 1:
                    emit_m2_ar_ckv()
            elif t == 1:
                for ch in range(ch0, ch0 + 4):
                    qkvb[ch].w(qkvb[ch].wr + list(qkvb[ch].rd))
                emit_rope(ch0, None, 4 + 2 * s)
                emit_rope(ch0 + 2, None, 5 + 2 * s)
                emit_q_stores([ch0, ch0 + 1, ch0 + 2, ch0 + 3])
    CC_Q = emit("gp", lambda g: g.collective_compute(
        "AllGather", ALU.bypass, replica_groups=[list(range(NCORE))],
        ins=[q_in], outs=[q_all]), deps=qst_toks, inc="cc")

    if PH < 4:
        _emit_output()
        return _finish()
    bar = snap()
    barn = Arena("att")
    Qa2 = [barn.al(f"Qa{i}", [128, 4096], BF16) for i in range(2)]
    Ka2 = [barn.al(f"Ka{i}", [128, 4096], BF16) for i in range(2)]
    Vt = barn.al("Vt", [128, 4096], BF16)
    Va2 = [barn.al(f"Va{i}", [128, 32 * 66], BF16) for i in range(2)]
    PT = [barn.al(f"PT{i}", [128, 512], BF16) for i in range(4)]
    ctxs = barn.al("ctxs", [128, 4096], BF16)
    ctxU = barn.al("ctxU", [128, 4096], F32)
    ctxtm = barn.al("ctxtm", [128, 2048], BF16)
    masksb = barn.al("masksb", [128, 2048], BF16)
    sq4 = barn.al("sq4", [128, 1024], BF16)
    tmpb = barn.al("tmpb", [128, 512], F32)
    rden = barn.al("rden", [128, 512], F32)
    assert barn.off <= SBUF_END
    bQd = [Buf(bar), Buf(bar)]
    bK2 = [Buf(bar), Buf(bar)]
    bVt = Buf(bar)
    bVa2 = [Buf(bar), Buf(bar)]
    bPT = [Buf(bar) for _ in range(4)]
    bctxq = [Buf(bar) for _ in range(8)]
    bcsq = [Buf(bar) for _ in range(8)]
    bctxtm = Buf(bar)
    bsq4 = [Buf(bar), Buf(bar)]
    btmpb = Buf(bar)
    brden = Buf(bar)
    bm2 = Buf(bar)

    MASK_TOK = emit("gp", lambda g: g.dma_start(out=masksb[:, :], in_=mask_d),
                    inc="gmsk", inc_n=16)
    tkn = emit("gp", lambda g: g.dma_start(out=m2s[0:1, 0:8], in_=ar_out),
               deps=[CCH["AR"]] + bm2.wdep(), inc="gm2", inc_n=16)
    tkn = emit("dve", lambda v: v.reduce_max(m2g[0:1, 0:1], m2s[0:1, 0:8], axis=AX.X),
               deps=[tkn], inc="dve")
    bm2.w(tkn)
    KROW_TOKS = [emit("dve", lambda v, rb=rb: v.memset(Ka2[rb][64:65, :], 1.0),
                      deps=bK2[rb].wdep(), inc="dve") for rb in range(2)]
    VONES_TOKS = [emit("dve", lambda v, rb=rb: v.memset(Va2[rb][:, :], 1.0),
                       deps=bVa2[rb].wdep(), inc="dve") for rb in range(2)]

    GKV = emit("gp", lambda g: g.dma_start(
        out=mykv.rearrange("(j r) c -> j r c", r=128),
        in_=kv_all.rearrange("(j r) c -> j r c", r=1024)[
            :, DynSlice(g.partition_id() * 128, 128), :]),
        deps=[CCH["KV"]], inc="gg", inc_n=16)
    mykv_sh = mykv.rearrange("(j r) c -> j r c", r=128)
    myq_sh = myq.rearrange("(j r) c -> j r c", r=256)

    GQB = []
    for b in range(2):
        rb = b
        ktoks = [KROW_TOKS[rb]]
        tkn = emit("gp", lambda g, b=b, rb=rb: g.dma_start(
            out=Ka2[rb][0:64, :].rearrange("p (j c) -> p j c", c=1024),
            in_=mykv_sh[4 * b:4 * b + 4, 0:64, :].rearrange("j r c -> r j c")),
            deps=[GKV] + bK2[rb].wdep(), inc=("gka" if rb == 0 else "gcp"), inc_n=16)
        ktoks.append(tkn)
        bK2[rb].w(ktoks)
        tkn = emit("gp", lambda g, b=b: g.dma_start(
            out=Vt[0:64, :].rearrange("p (j c) -> p j c", c=1024),
            in_=mykv_sh[4 * b:4 * b + 4, 64:128, :].rearrange("j r c -> r j c")),
            deps=[GKV] + bVt.wdep(), inc="gvt", inc_n=16)
        bVt.w(tkn)
        va_toks = [VONES_TOKS[rb]]
        for kc in range(32):
            tk = transp(lambda kc=kc: Vt[0:64, kc * 128:(kc + 1) * 128],
                        Va2[rb], kc * 66, kc * 66 + 64,
                        bVt.rdep() + bVa2[rb].wdep() + [VONES_TOKS[rb]], rows=64)
            bVt.r(tk)
            va_toks.append(tk)
        bVa2[rb].w(va_toks)
    for bb in range(2):
        tkn = emit("gp", lambda g, bb=bb: g.dma_start(
            out=myq.rearrange("(j r) c -> j r c", r=256)[4 * bb:4 * bb + 4],
            in_=q_all.rearrange("(j r) c -> j r c", r=2048)[
                4 * bb:4 * bb + 4, DynSlice(g.partition_id() * 256, 256), :]),
            deps=[CC_Q], inc=f"gqa{bb}", inc_n=16)
        GQB.append(tkn)

    heads = [(b, hq) for b in range(2) for hq in range(4)]
    qload = [None, None]
    qrow = [[None] * 8, [None] * 8]
    qpend = [None, None]

    def emit_qa_load(i):
        b, hq = heads[i]
        r = i % 2
        Qh = Qa2[r]
        tokq = emit("gp", lambda g, b=b, hq=hq, Qh=Qh: g.dma_start(
            out=Qh[0:64, :].rearrange("p (j c) -> p j c", c=1024),
            in_=myq_sh[4 * b:4 * b + 4, 64 * hq:64 * hq + 64, :].rearrange(
                "j r c -> r j c")),
            deps=[GQB[b]] + bQd[r].wdep(), inc="gqa", inc_n=16)
        qload[r] = tokq
        qpend[r] = [tokq]

    def emit_bound_sq(i, chn):
        r = i % 2
        Qh = Qa2[r]
        rr = chn % 2
        tks = emit("dve", lambda v, chn=chn, rr=rr, Qh=Qh: v.tensor_mul(
            sq4[0:64, rr * 512:(rr + 1) * 512],
            Qh[0:64, chn * 512:(chn + 1) * 512],
            Qh[0:64, chn * 512:(chn + 1) * 512]),
            deps=[qload[r]] + bsq4[rr].wdep(), inc="dve")
        bsq4[rr].w(tks)

    def emit_bound_mm(i, chn):
        r = i % 2
        Qh = Qa2[r]
        rr = chn % 2
        bank = 4 + rr
        tkm = emit("pe", lambda tt_, rr=rr, bank=bank: tt_.matmul(
            pbk[bank][0:1, :], onesb[0:64, 0:1], sq4[0:64, rr * 512:(rr + 1) * 512],
            start=True, stop=True),
            deps=bsq4[rr].rdep() + pbuf[bank].wdep() + [MEMSET_TOK], inc="pe")
        pbuf[bank].w(tkm)
        bsq4[rr].r(tkm)
        tkn = emit("dve", lambda v, bank=bank: v.tensor_scalar(
            tmpb[0:1, :], pbk[bank][0:1, :], m2g[0:1, 0:1], None, ALU.mult),
            deps=pbuf[bank].rdep() + bm2.rdep() + btmpb.wdep(), inc="dve")
        pbuf[bank].r(tkn)
        bm2.r(tkn)
        tkn = emit("act", lambda a: a.activation(
            tmpb[0:1, :], tmpb[0:1, :], AF.Sqrt, bias=zb[0:1, :]),
            deps=[tkn, MEMSET_TOK], inc="act")
        btmpb.w(tkn)
        tkn = emit("dve", lambda v, chn=chn, Qh=Qh: v.tensor_scalar_mul(
            Qh[64:65, chn * 512:(chn + 1) * 512], tmpb[0:1, :], -1.01),
            deps=btmpb.rdep(), inc="dve")
        btmpb.r(tkn)
        qrow[r][chn] = tkn
        qpend[r].append(tkn)

    def emit_qa_bound_all(i):
        emit_qa_load(i)
        for chn in range(8):
            emit_bound_sq(i, chn)
            emit_bound_mm(i, chn)
        bQd[i % 2].w(qpend[i % 2])

    for b in range(2):
        rb = b
        Ka = Ka2[rb]
        Va = Va2[rb]
        bK = bK2[rb]
        bVa = bVa2[rb]

        for hq in range(4):
            i = 4 * b + hq
            r = i % 2
            Qh = Qa2[r]
            if i == 0:
                emit_qa_bound_all(0)

            tm_toks = []
            pend_bc = [None]
            pend_tp = [None]

            def mk_norm(qb):
                def do_bc():
                    tok_bc = emit("pe", lambda tt_, qb=qb: tt_.matmul(
                        pbk[3][0:64, :], onesf1[64:65, :],
                        ctxU[64:65, qb * 512:(qb + 1) * 512], start=True, stop=True),
                        deps=bctxq[qb].rdep() + pbuf[3].wdep() + [MEMSET_TOK], inc="pe")
                    pbuf[3].w(tok_bc)
                    bctxq[qb].r(tok_bc)
                    tkr = emit("dve", lambda v: v.reciprocal(rden[0:64, :], pbk[3][0:64, :]),
                               deps=pbuf[3].rdep() + brden.wdep(), inc="dve")
                    pbuf[3].r(tkr)
                    brden.w(tkr)
                    tkn = emit("dve", lambda v, qb=qb: v.tensor_mul(
                        ctxs[0:64, qb * 512:(qb + 1) * 512],
                        ctxU[0:64, qb * 512:(qb + 1) * 512], rden[0:64, :]),
                        deps=brden.rdep() + bctxq[qb].rdep() + bcsq[qb].wdep(), inc="dve")
                    brden.r(tkn)
                    bctxq[qb].r(tkn)
                    bcsq[qb].w(tkn)

                def do_tp():
                    for kc4 in range(4):
                        kc = qb * 4 + kc4
                        tk = transp(lambda kc=kc: ctxs[0:64, kc * 128:(kc + 1) * 128],
                                    ctxtm, kc * 64, (kc + 1) * 64,
                                    bcsq[qb].rdep() + bctxtm.wdep(), rows=64)
                        bcsq[qb].r(tk)
                        tm_toks.append(tk)
                return do_bc, do_tp

            def flush_pend():
                if pend_bc[0] is not None:
                    pend_bc[0]()
                    pend_bc[0] = None
                if pend_tp[0] is not None:
                    pend_tp[0]()
                    pend_tp[0] = None

            for qb in range(8):
                nkc = 4 * (qb + 1)

                def emit_pv(kc, nkc=nkc):
                    pslot = kc % 4
                    last = (kc == nkc - 1)
                    tok_pv = emit("pe", lambda tt_, kc=kc, pslot=pslot, nkc=nkc, Va=Va: tt_.matmul(
                        pbk[2][0:65, :], Va[:, kc * 66:kc * 66 + 65], PT[pslot][:, :],
                        start=(kc == 0), stop=(kc == nkc - 1)),
                        deps=bPT[pslot].rdep() + bVa.rdep() +
                        (pbuf[2].wdep() if kc == 0 else []),
                        inc=("pe" if last else None))
                    bPT[pslot].r(tok_pv)
                    bVa.r(tok_pv)

                for kc in range(nkc):
                    sbank = kc % 2
                    dj = kc - 4 * qb
                    if kc == 2 and pend_bc[0] is not None:
                        pend_bc[0]()
                        pend_bc[0] = None
                    if kc == 5 and pend_tp[0] is not None:
                        pend_tp[0]()
                        pend_tp[0] = None
                    deps = bK.rdep() + [qload[r], qrow[r][qb]] + pbuf[sbank].wdep()
                    tok_s = emit("pe", lambda tt_, kc=kc, qb=qb, sbank=sbank, Qh=Qh, Ka=Ka: tt_.matmul(
                        pbk[sbank][:, :], Ka[0:65, kc * 128:(kc + 1) * 128],
                        Qh[0:65, qb * 512:(qb + 1) * 512], start=True, stop=True),
                        deps=deps, inc="pe")
                    pbuf[sbank].w(tok_s)
                    bK.r(tok_s)
                    bQd[r].r(tok_s)
                    pslot = kc % 4
                    tok_e = emit("act", lambda a, pslot=pslot, sbank=sbank: a.activation(
                        PT[pslot][:, :], pbk[sbank][:, :], AF.Exp, bias=zb[:, :], scale=0.125),
                        deps=pbuf[sbank].rdep() + bPT[pslot].wdep() + [MEMSET_TOK], inc="act")
                    pbuf[sbank].r(tok_e)
                    if dj >= 0:
                        tok_m = emit("dve", lambda v, pslot=pslot, dj=dj: v.tensor_mul(
                            PT[pslot][:, :], PT[pslot][:, :],
                            masksb[:, dj * 512:(dj + 1) * 512]),
                            deps=[tok_e, MASK_TOK], inc="dve")
                        bPT[pslot].w([tok_e, tok_m])
                    else:
                        bPT[pslot].w(tok_e)
                    if kc >= 1:
                        emit_pv(kc - 1)
                emit_pv(nkc - 1)
                tok_ctx = ("pe", cnt["pe"])
                pbuf[2].w(tok_ctx)
                tkc = emit("dve", lambda v, qb=qb: v.tensor_copy(
                    ctxU[0:65, qb * 512:(qb + 1) * 512], pbk[2][0:65, :]),
                    deps=[tok_ctx] + bctxq[qb].wdep(), inc="dve")
                pbuf[2].r(tkc)
                bctxq[qb].w(tkc)
                flush_pend() if qb == 7 else None
                pend_bc[0], pend_tp[0] = mk_norm(qb)
                if BOUND_SPREAD:
                    if i + 1 < 8:
                        if qb == 0:
                            emit_qa_load(i + 1)
                            emit_bound_sq(i + 1, 0)
                            emit_bound_sq(i + 1, 1)
                        else:
                            emit_bound_mm(i + 1, qb - 1)
                            if qb + 1 < 8:
                                emit_bound_sq(i + 1, qb + 1)
                elif qb == 3 and i + 1 < 8:
                    emit_qa_bound_all(i + 1)
            if BOUND_SPREAD and i + 1 < 8:
                emit_bound_mm(i + 1, 7)
                bQd[(i + 1) % 2].w(qpend[(i + 1) % 2])
            flush_pend()
            bctxtm.w(tm_toks)
            tkn = emit("sp", lambda s, b=b, hq=hq: s.dma_start(
                out=ctx_in.rearrange("(bb kc p) d -> bb p kc d", bb=2, p=128)[
                    b, :, :, hq * 64:(hq + 1) * 64],
                in_=ctxtm[:, :].rearrange("p (kc d) -> p kc d", d=64)),
                deps=bctxtm.rdep(), inc="scs", inc_n=16)
            bctxtm.r(tkn)

    CC_CTX = emit("gp", lambda g: g.collective_compute(
        "AllGather", ALU.bypass, replica_groups=[list(range(NCORE))],
        ins=[ctx_in], outs=[ctx_all]), deps=[("scs", cnt["scs"])], inc="cc")

    if PH < 5:
        _emit_output()
        return _finish()
    bar = snap()
    car = Arena("wo", start=ARENA + 26624)
    ctxld = car.al("ctxld", [128, 16 * 1024], BF16)
    cstg = [car.al(f"cstg{i}", [128, 2048], BF16) for i in range(2)]
    bctxg = [[Buf(bar), Buf(bar)] for _ in range(NDC)]
    bcstg = [Buf(bar), Buf(bar)]
    GC = emit("gp", lambda g: g.dma_start(
        out=myctx.rearrange("(s r) c -> s r c", r=1024),
        in_=ctx_all.rearrange("(s r) c -> s r c", r=8192)[
            :, DynSlice(g.partition_id() * 1024, 1024), :]),
        deps=[CC_CTX], inc="gg", inc_n=16)
    for tt in range(8):
        r = tt % 2
        tkn = emit("gp", lambda g, tt=tt, r=r: g.dma_start(
            out=cstg[r][:, :].rearrange("p (s c) -> p s c", c=256),
            in_=myctx.rearrange("(s r) c -> s r c", r=1024)[
                :, tt * 128:(tt + 1) * 128, :].rearrange("s r c -> r s c")),
            deps=[GC] + bcstg[r].wdep(), inc=f"gc{r}", inc_n=16)
        bcstg[r].w(tkn)
        for dc in range(NDC):
            tk = transp(lambda dc=dc, r=r: cstg[r][:, dc * 128:(dc + 1) * 128],
                        ctxld, dc * 1024 + tt * 128, dc * 1024 + (tt + 1) * 128,
                        bcstg[r].rdep() + bctxg[dc][tt // 4].wdep())
            bcstg[r].r(tk)
            bctxg[dc][tt // 4].r(tk)
    for dc in range(NDC):
        for tl in range(2):
            bctxg[dc][tl].w(list(bctxg[dc][tl].rd))

    far2 = Arena("f2")
    fscr2 = None
    for t in range(2):
        for s in range(4):
            if t == 1 and s == 0 and fscr2 is None:
                fscr2 = mk_ln_scr(far2, snap())
                emit_ln_stats(0, fscr2)
            bq = (s % 2) * 2
            tokw = emit("sp", load_proj_slice("o", s, bq),
                        deps=wq4[bq].wdep() + wq4[bq + 1].wdep() + [CC_WO],
                        inc=f"sw{bq}", inc_n=16)
            wq4[bq].w(tokw)
            wq4[bq + 1].w(tokw)
            for ol in range(4):
                oc = 4 * s + ol
                bank = ol % 2
                deps = [tokw] + pbuf[bank].wdep()
                tokm = None
                for dc in range(NDC):
                    tokm = emit("pe", lambda tt_, dc=dc, ol=ol, bq=bq, bank=bank, t=t: tt_.matmul(
                        pbk[bank][:, :],
                        wbuf[:, bq * 4096 + dc * 512 + ol * 128: bq * 4096 + dc * 512 + (ol + 1) * 128],
                        ctxld[:, dc * 1024 + t * 512: dc * 1024 + t * 512 + 512],
                        start=(dc == 0), stop=(dc == NDC - 1)),
                        deps=(deps if dc == 0 else []) + bctxg[dc][t].rdep(),
                        inc=("pe" if dc == NDC - 1 else None))
                pbuf[bank].w(tokm)
                wq4[bq].r(tokm)
                wq4[bq + 1].r(tokm)
                for dc in range(NDC):
                    bctxg[dc][t].r(tokm)
                xs = x_sb[:, oc * 1024 + t * 512: oc * 1024 + t * 512 + 512]
                tokev = emit("dve", lambda v, xs=xs, bank=bank: v.tensor_add(
                    xs, pbk[bank][:, :], xs),
                    deps=pbuf[bank].rdep() + xch[oc][t].wdep(), inc="dve")
                pbuf[bank].r(tokev)
                xch[oc][t].w(tokev)

    bar = snap()
    g_sb2 = far2.al("g", [128, 64 * 512], BF16)
    gbufs2 = [Buf(bar) for _ in range(NHC)]
    emit_ln_normalize(0, fscr2)
    emit_gate_up(0, fscr2, gbufs2, g_sb2)
    emit_layernorm(1, fscr2)
    emit_down(0, fscr2, gbufs2, g_sb2)
    emit_gate_up(1, fscr2, gbufs2, g_sb2)
    emit_down(1, fscr2, gbufs2, g_sb2)

    _emit_output()
    return _finish()


_NC_CACHE = {}


def kernel(L0_attn_x, L0_attn_Wq, L0_attn_Wk, L0_attn_Wv, L0_attn_Wo,
           L0_ffn_w_gate, L0_ffn_w_up, L0_ffn_w_down):
    x = np.ascontiguousarray(np.asarray(L0_attn_x, dtype=np.float32).reshape(8192, 2048))
    Wq = np.ascontiguousarray(np.asarray(L0_attn_Wq, dtype=np.float32))
    Wk = np.ascontiguousarray(np.asarray(L0_attn_Wk, dtype=np.float32))
    Wv = np.ascontiguousarray(np.asarray(L0_attn_Wv, dtype=np.float32))
    Wo = np.ascontiguousarray(np.asarray(L0_attn_Wo, dtype=np.float32))
    wg = np.asarray(L0_ffn_w_gate, dtype=np.float32)
    wu = np.asarray(L0_ffn_w_up, dtype=np.float32)
    wd = np.asarray(L0_ffn_w_down, dtype=np.float32)
    cos_t, sin_t = rope_tables()

    if "nc" not in _NC_CACHE:
        _NC_CACHE["nc"] = build_nc()
    nc = _NC_CACHE["nc"]

    in_maps = []
    for c in range(NCORE):
        in_maps.append({
            "x_slice": np.ascontiguousarray(x[c * 1024:(c + 1) * 1024]),
            "wq_sh": np.ascontiguousarray(Wq[c * 256:(c + 1) * 256]),
            "wk_sh": np.ascontiguousarray(Wk[c * 256:(c + 1) * 256]),
            "wv_sh": np.ascontiguousarray(Wv[c * 256:(c + 1) * 256]),
            "wo_sh": np.ascontiguousarray(Wo[c * 256:(c + 1) * 256]),
            "wg_sh": np.ascontiguousarray(wg[:, c * 256:(c + 1) * 256]),
            "wu_sh": np.ascontiguousarray(wu[:, c * 256:(c + 1) * 256]),
            "wd_sh": np.ascontiguousarray(wd[:, c * 1024:(c + 1) * 1024]),
            "rope_cos": cos_t[c % 4],
            "rope_sin2": sin_t[c % 4],
        })
    res = run_bass_kernel_spmd(nc, in_maps, core_ids=list(range(NCORE)))
    out = np.concatenate([r["out_slice"] for r in res.results], axis=0)
    return out.reshape(2, 4096, 2048)


# version 19
# speedup vs baseline: 1.0163x; 1.0163x over previous
import numpy as np
import ml_dtypes
import concourse.bass as bass
import concourse.mybir as mybir
from concourse.bass_types import DynSlice
from concourse.bass_utils import run_bass_kernel_spmd

F32 = mybir.dt.float32
BF16 = mybir.dt.bfloat16
AF = mybir.ActivationFunctionType
ALU = mybir.AluOpType
AX = mybir.AxisListType
bfdt = ml_dtypes.bfloat16

NCORE = 8
BOUND_SPREAD = True
D = 2048
HID = 8192
TOKC = 1024
NDC = D // 128
NHC = HID // 128
T = 4096
MASKV = -30000.0
SCRATCH = 28672

SEMS = ("pe", "dve", "act", "gp", "cc",
        "sw0", "sw1", "sw2", "sw3", "sx0", "sx1", "sqs", "scs", "sos0", "sos1",
        "sa0", "sa1", "sg0", "sg1", "sd0", "sd1", "sd2", "sd3",
        "gconst", "gstage", "ga0", "ga1", "grope", "gg", "gka", "gvt", "gqa",
        "gmsk", "gm2", "gc0", "gc1", "gcp", "gqa0", "gqa1", "skv",
        "gr0", "gr1", "gr2", "gr3", "gr4", "gr5", "gr6", "gr7", "gr8", "gr9",
        "gr10", "gr11")
ENGS = ("pe", "dve", "act", "gp", "sp")
OWN = {"pe": "pe", "dve": "dve", "act": "act", "gp": "gp", "sp": None}


class Buf:
    __slots__ = ("wr", "rd")

    def __init__(self, init=()):
        self.wr = []
        self.rd = list(init)

    def rdep(self):
        return list(self.wr)

    def wdep(self):
        return list(self.wr) + list(self.rd)

    def w(self, toks):
        if isinstance(toks, tuple):
            toks = [toks]
        self.wr = [t for t in toks if t is not None]
        self.rd = []

    def r(self, tok):
        if tok is not None:
            self.rd.append(tok)


def host_constants():
    c = {}
    c["idenf"] = np.eye(128, dtype=np.float32)
    c["idenb"] = np.eye(128, dtype=bfdt)
    c["onesb"] = np.ones((128, 128), dtype=bfdt)
    indk = np.zeros((128, 4 * 8), dtype=bfdt)
    for kc in range(4):
        indk[0:64, kc * 8 + 2 * kc] = 1
        indk[64:128, kc * 8 + 2 * kc + 1] = 1
    c["indk"] = indk
    rep = np.zeros((8, 128), dtype=bfdt)
    for m in range(128):
        rep[m % 8, m] = 1
    c["repmap"] = rep
    msk = np.zeros((128, 4 * 512), dtype=bfdt)
    kk = np.arange(128)[:, None]
    qq = np.arange(512)[None, :]
    for j in range(4):
        msk[:, j * 512:(j + 1) * 512] = np.where(kk + 128 * j > qq, 0.0, 1.0).astype(bfdt)
    c["masks"] = msk
    return c


def rope_tables():
    ex = (np.arange(0, 64, 2, dtype=np.float32) / np.float32(64.0)).astype(np.float32)
    th = (np.float32(1.0) / np.power(np.float32(10000.0), ex)).astype(np.float32)
    pos = np.arange(T, dtype=np.float32)
    freqs = (pos[:, None] * th[None, :]).astype(np.float32)
    cos = np.cos(freqs.astype(np.float64)).astype(np.float32)
    sin = np.sin(freqs.astype(np.float64)).astype(np.float32)
    cos_t, sin_t = [], []
    for q in range(4):
        cb = cos[q * 1024:(q + 1) * 1024, :].T
        sb = sin[q * 1024:(q + 1) * 1024, :].T
        cos_t.append(np.ascontiguousarray(np.tile(cb, (4, 1))))
        sin_t.append(np.ascontiguousarray(np.concatenate([sb, -sb, sb, -sb], axis=0)))
    return cos_t, sin_t


def build_nc(phases=5, use_ag=None):
    if use_ag is None:
        use_ag = True
    nc = bass.Bass("TRN2", target_bir_lowering=False, num_devices=NCORE,
                   dynamic_dma_scratch_size=SCRATCH,
                   detect_race_conditions=False)

    x_in = nc.dram_tensor("x_slice", [TOKC, D], F32, kind="ExternalInput").ap()
    wq_sh = nc.dram_tensor("wq_sh", [256, 2048], F32, kind="ExternalInput").ap()
    wk_sh = nc.dram_tensor("wk_sh", [256, 512], F32, kind="ExternalInput").ap()
    wv_sh = nc.dram_tensor("wv_sh", [256, 512], F32, kind="ExternalInput").ap()
    wo_sh = nc.dram_tensor("wo_sh", [256, 2048], F32, kind="ExternalInput").ap()
    wg_sh = nc.dram_tensor("wg_sh", [8192, 256], F32, kind="ExternalInput").ap()
    wu_sh = nc.dram_tensor("wu_sh", [8192, 256], F32, kind="ExternalInput").ap()
    wd_sh = nc.dram_tensor("wd_sh", [2048, 1024], F32, kind="ExternalInput").ap()
    ropec = nc.dram_tensor("rope_cos", [128, 1024], F32, kind="ExternalInput").ap()
    ropes = nc.dram_tensor("rope_sin2", [128, 1024], F32, kind="ExternalInput").ap()
    out_d = nc.dram_tensor("out_slice", [TOKC, D], F32, kind="ExternalOutput").ap()

    wg_in = nc.dram_tensor("wg_in", [256, 8192], BF16).ap()
    wu_in = nc.dram_tensor("wu_in", [256, 8192], BF16).ap()
    wdt_in = nc.dram_tensor("wdt_in", [1024, 2048], BF16).ap()
    wqc_in = nc.dram_tensor("wqc_in", [256, 2048], BF16).ap()
    woc_in = nc.dram_tensor("woc_in", [256, 2048], BF16).ap()
    wkc_in = nc.dram_tensor("wkc_in", [256, 512], BF16).ap()
    wvc_in = nc.dram_tensor("wvc_in", [256, 512], BF16).ap()
    ag_wg = nc.dram_tensor("ag_wg", [2048, 8192], BF16, addr_space="Shared").ap()
    ag_wu = nc.dram_tensor("ag_wu", [2048, 8192], BF16, addr_space="Shared").ap()
    ag_wd = nc.dram_tensor("ag_wd", [8192, 2048], BF16, addr_space="Shared").ap()
    ag_wq = nc.dram_tensor("ag_wq", [2048, 2048], BF16, addr_space="Shared").ap()
    ag_wo = nc.dram_tensor("ag_wo", [2048, 2048], BF16, addr_space="Shared").ap()
    ag_wk = nc.dram_tensor("ag_wk", [2048, 512], BF16, addr_space="Shared").ap()
    ag_wv = nc.dram_tensor("ag_wv", [2048, 512], BF16, addr_space="Shared").ap()
    kv_in = nc.dram_tensor("kv_in", [1024, 1024], BF16).ap()
    kv_all = nc.dram_tensor("kv_all", [8 * 1024, 1024], BF16, addr_space="Shared").ap()
    q_in = nc.dram_tensor("q_in", [2048, 1024], BF16).ap()
    q_all = nc.dram_tensor("q_all", [8 * 2048, 1024], BF16, addr_space="Shared").ap()
    ar_in = nc.dram_tensor("ar_in", [1, 8], F32).ap()
    ar_out = nc.dram_tensor("ar_out", [1, 8], F32).ap()
    ctx_in = nc.dram_tensor("ctx_in", [8192, 256], BF16).ap()
    ctx_all = nc.dram_tensor("ctx_all", [8 * 8192, 256], BF16, addr_space="Shared").ap()
    myq = nc.dram_tensor("myq", [8 * 256, 1024], BF16).ap()
    mykv = nc.dram_tensor("mykv", [8 * 128, 1024], BF16).ap()
    myctx = nc.dram_tensor("myctx", [8 * 1024, 256], BF16).ap()

    hc_ = host_constants()
    idenf_d = nc.inline_tensor(hc_["idenf"], name="idenf_c").ap()
    idenb_d = nc.inline_tensor(hc_["idenb"].astype(np.float32), name="idenb_c").ap()
    onesb_d = nc.inline_tensor(hc_["onesb"].astype(np.float32), name="onesb_c").ap()
    indk_d = nc.inline_tensor(hc_["indk"].astype(np.float32), name="indk_c").ap()
    rep_d = nc.inline_tensor(hc_["repmap"].astype(np.float32), name="repmap_c").ap()
    mask_d = nc.inline_tensor(hc_["masks"].astype(np.float32), name="masks_c").ap()

    def sb(name, shape, dtype, off):
        return nc.alloc_sbuf_tensor_at(name, shape, dtype, offset=off)

    off = SCRATCH + 2048

    def sbn(name, shape, dtype):
        nonlocal off
        t = sb(name, shape, dtype, off)
        nbytes = int(np.prod(shape[1:])) * (2 if dtype == BF16 else 4)
        off += (nbytes + 63) & ~63
        return t

    x_sb = sbn("x_sb", [128, 16 * 1024], F32)
    wbuf = sbn("wbuf", [128, 16384], BF16)
    idenf = sbn("idenf", [128, 128], F32)
    idenb = sbn("idenb", [128, 128], BF16)
    onesb = sbn("onesb", [128, 128], BF16)
    indk = sbn("indk", [128, 32], BF16)
    repmap = sbn("repmap", [8, 128], BF16)
    eps1 = sbn("eps1", [128, 1], F32)
    zb = sbn("zb", [128, 1], F32)
    onesf1 = sbn("onesf1", [128, 64], F32)
    m2s = sbn("m2s", [128, 8], F32)
    m2g = sbn("m2g", [128, 1], F32)
    ARENA = off
    SBUF_END = 229376 - 32
    assert ARENA <= 139100, ARENA

    class Arena:
        def __init__(self, tag, start=None):
            self.off = ARENA if start is None else start
            self.tag = tag

        def al(self, name, shape, dtype):
            t = sb(f"{name}_{self.tag}", shape, dtype, self.off)
            nbytes = int(np.prod(shape[1:])) * (2 if dtype == BF16 else 4)
            self.off += (nbytes + 63) & ~63
            assert self.off <= SBUF_END, (name, self.tag, self.off)
            return t

    pbk = [nc.alloc_psum_tensor(f"pb{i}", [128, 512], F32) for i in range(6)]
    tbb = [nc.alloc_psum_tensor(f"tbb{i}", [128, 1024], BF16) for i in range(2)]

    from contextlib import ExitStack
    es = ExitStack()
    sems = {s: es.enter_context(nc.semaphore(f"sem_{s}")) for s in SEMS}
    st = {e: [] for e in ENGS}
    cnt = {s: 0 for s in SEMS}
    mark = {}

    def emit(e, fn, deps=(), inc=None, inc_n=1):
        m = {}
        for tkn in deps:
            if tkn is None:
                continue
            s, v = tkn
            if v and m.get(s, 0) < v:
                m[s] = v
        for s in sorted(m):
            v = m[s]
            if s == OWN[e]:
                continue
            key = (e, s)
            if mark.get(key, 0) >= v:
                continue
            mark[key] = v
            st[e].append(lambda eng, s=s, v=v: eng.wait_ge(sems[s], v))
        if inc is None:
            st[e].append(fn)
            return None
        cnt[inc] += inc_n
        st[e].append(lambda eng, fn=fn, s=inc, n=inc_n: fn(eng).then_inc(sems[s], n))
        return (inc, cnt[inc])

    def snap():
        return [(s, cnt[s]) for s in SEMS if s != "cc"]

    pbuf = [Buf() for _ in range(8)]
    xch = [[Buf() for _ in range(2)] for _ in range(NDC)]
    wq4 = [Buf() for _ in range(4)]

    for dst, src in [(idenf, idenf_d), (idenb, idenb_d), (onesb, onesb_d),
                     (indk, indk_d), (repmap, rep_d)]:
        tkn = emit("gp", lambda g, dst=dst, src=src: g.dma_start(out=dst[:, :], in_=src),
                   inc="gconst", inc_n=16)
    CONSTS_TOK = tkn
    emit("dve", lambda v: v.memset(eps1[:, :], 1e-5))
    emit("dve", lambda v: v.memset(zb[:, :], 0.0))
    MEMSET_TOK = emit("dve", lambda v: v.memset(onesf1[:, :], 1.0), inc="dve")

    par = Arena("pro")
    stage_in2 = [par.al(f"stage_in{i}", [128, 8192], BF16) for i in range(2)]
    wgs0 = par.al("wgs0", [128, 8192], BF16)
    wgs1 = par.al("wgs1", [128, 8192], BF16)
    wds = [par.al(f"wds{i}", [128, 2048], BF16) for i in range(2)]
    xio = [par.al(f"xio{i}", [128, 2048], F32) for i in range(2)]
    b_stage2 = [Buf(), Buf()]
    b_wgs = [Buf(), Buf()]
    b_wds = [Buf() for _ in range(2)]
    xiob = [Buf(), Buf()]

    tp_rot = [0]
    tbbuf = [Buf(), Buf()]

    def transp(in_ap_fn, out_sb, c0, c1, deps, fp32=False, rows=128):
        slot = tp_rot[0] % 2
        tp_rot[0] += 1
        iden = idenf if fp32 else idenb
        pt = pbk[4 + slot] if fp32 else tbb[slot]
        pbf = pbuf[4 + slot] if fp32 else tbbuf[slot]
        tkn = emit("pe", lambda tt, f=in_ap_fn, iden=iden, rows=rows, pt=pt: tt.transpose(
            pt[0:128, 0:rows], f(), iden[0:rows, 0:rows]),
            deps=list(deps) + pbf.wdep() + [CONSTS_TOK], inc="pe")
        pbf.w(tkn)
        tkn2 = emit("dve", lambda v, o=out_sb, c0=c0, c1=c1, pt=pt: v.tensor_copy(
            o[:, c0:c1], pt[0:128, 0:c1 - c0]), deps=pbf.rdep(), inc="dve")
        pbf.r(tkn2)
        return tkn2

    SKIP_PRO = phases <= -2
    SKIP_AG = phases <= -1
    ag1_store_toks = []
    nast = 0
    SKIP_PRO = phases <= -2
    SKIP_AG = phases <= -1
    for src, dst, r0, w in ([] if SKIP_PRO else [(wq_sh, wqc_in, 0, 2048), (wo_sh, woc_in, 0, 2048),
                            (wk_sh, wkc_in, 0, 512), (wv_sh, wvc_in, 0, 512)]):
        for half in range(2):
            r = nast % 2
            nast += 1
            tkn = emit("gp", lambda g, src=src, half=half, r=r, w=w: g.dma_start(
                out=stage_in2[r][:, 0:w], in_=src[half * 128:(half + 1) * 128, :]),
                deps=b_stage2[r].wdep(), inc=f"ga{r}", inc_n=16)
            b_stage2[r].w(tkn)
            tkn2 = emit("sp", lambda s, dst=dst, r0=r0, half=half, r=r, w=w: s.dma_start(
                out=dst[r0 + half * 128:r0 + (half + 1) * 128, :], in_=stage_in2[r][:, 0:w]),
                deps=b_stage2[r].rdep(), inc=f"sa{r}", inc_n=16)
            b_stage2[r].r(tkn2)
            ag1_store_toks.append(tkn2)

    gu_ag_toks = {}
    for src, gu_dst in ([] if SKIP_PRO else [(wg_sh, wg_in), (wu_sh, wu_in)]):
        w_toks = []
        for gidx in range(8):
            sg = gidx % 2
            stg = stage_in2[sg]
            tkn = emit("gp", lambda g, src=src, gi=gidx, stg=stg: g.dma_start(
                out=stg[:, 0:2048].rearrange("p (j c) -> p j c", c=256),
                in_=src[gi * 1024:(gi + 1) * 1024, :].rearrange("(j p) c -> p j c", p=128)),
                deps=b_stage2[sg].wdep(), inc="gstage", inc_n=16)
            b_stage2[sg].w(tkn)
            for j in range(8):
                rg = gidx * 8 + j
                for dh in range(2):
                    tk = transp(lambda j=j, dh=dh, stg=stg: stg[:, j * 256 + dh * 128: j * 256 + (dh + 1) * 128],
                                wgs0 if dh == 0 else wgs1, rg * 128, (rg + 1) * 128,
                                b_stage2[sg].rdep() + b_wgs[dh].wdep())
                    b_stage2[sg].r(tk)
                    b_wgs[dh].r(tk)
        for dh in range(2):
            b_wgs[dh].w(list(b_wgs[dh].rd))
            tkn2 = emit("sp", lambda s, gu_dst=gu_dst, dh=dh: s.dma_start(
                out=gu_dst[dh * 128:(dh + 1) * 128, :],
                in_=(wgs0 if dh == 0 else wgs1)[:, :]),
                deps=b_wgs[dh].rdep(), inc=f"sg{dh}", inc_n=16)
            b_wgs[dh].r(tkn2)
            w_toks.append(tkn2)
        gu_ag_toks[gu_dst.tensor.name] = list(w_toks)

    wd_toks = []
    for hhalf in ([] if SKIP_PRO else range(4)):
        for gidx in range(2):
            sg = gidx % 2
            stg = stage_in2[sg]
            tkn = emit("gp", lambda g, gi=gidx, stg=stg: g.dma_start(
                out=stg[:, :].rearrange("p (j c) -> p j c", c=1024),
                in_=wd_sh[gi * 1024:(gi + 1) * 1024, :].rearrange("(j p) c -> p j c", p=128)),
                deps=b_stage2[sg].wdep(), inc="gstage", inc_n=16)
            b_stage2[sg].w(tkn)
            for j in range(8):
                dc = gidx * 8 + j
                for hl in range(2):
                    hcl = hhalf * 2 + hl
                    tk = transp(lambda j=j, hcl=hcl, stg=stg: stg[:, j * 1024 + hcl * 128: j * 1024 + (hcl + 1) * 128],
                                wds[hl], dc * 128, (dc + 1) * 128,
                                b_stage2[sg].rdep() + b_wds[hl].wdep())
                    b_stage2[sg].r(tk)
                    b_wds[hl].r(tk)
        for hl in range(2):
            hcl = hhalf * 2 + hl
            b_wds[hl].w(list(b_wds[hl].rd))
            tkn2 = emit("sp", lambda s, hcl=hcl, hl=hl: s.dma_start(
                out=wdt_in[hcl * 128:(hcl + 1) * 128, :], in_=wds[hl][:, :]),
                deps=b_wds[hl].rdep(), inc=f"sd{hl}", inc_n=16)
            b_wds[hl].r(tkn2)
            wd_toks.append(tkn2)

    def ag(in_t, out_t, deps=()):
        return emit("gp", lambda g, in_t=in_t, out_t=out_t: g.collective_compute(
            "AllGather", ALU.bypass, replica_groups=[list(range(NCORE))],
            ins=[in_t], outs=[out_t]), deps=deps, inc="cc")
    if not SKIP_AG:
        CC_WG = ag(wg_in, ag_wg, gu_ag_toks.get("wg_in", []))
        CC_WU = ag(wu_in, ag_wu, gu_ag_toks.get("wu_in", []))
        CC_WD = ag(wdt_in, ag_wd, wd_toks)
        CC_WQ = ag(wqc_in, ag_wq, ag1_store_toks)
        CC_WK = ag(wkc_in, ag_wk)
        CC_WV = ag(wvc_in, ag_wv)
        CC_WO = ag(woc_in, ag_wo)

    for tt in range(8):
        r = tt % 2
        tloc = tt // 4
        tkn = emit("sp", lambda s, tt=tt, r=r: s.dma_start(
            out=xio[r][:, :], in_=x_in[tt * 128:(tt + 1) * 128, :]),
            deps=xiob[r].wdep(), inc=f"sx{r}", inc_n=16)
        xiob[r].w(tkn)
        for dc in range(NDC):
            tk = transp(lambda dc=dc, r=r: xio[r][:, dc * 128:(dc + 1) * 128],
                        x_sb, dc * 1024 + tt * 128, dc * 1024 + (tt + 1) * 128,
                        xiob[r].rdep() + xch[dc][tloc].wdep(), fp32=True)
            xiob[r].r(tk)
            xch[dc][tloc].r(tk)
    for dc in range(NDC):
        for tl in range(2):
            xch[dc][tl].w(list(xch[dc][tl].rd))

    def emit_ln_stats(t, scr):
        xb2, xq2, mean, var, rstd = (scr["xb2"], scr["xq2"], scr["mean"],
                                     scr["var"], scr["rstd"])
        bxb, bxq = scr["bxb"], scr["bxq"]
        bmean, bvar, brstd = scr["bmean"], scr["bvar"], scr["brstd"]
        for dc in range(NDC):
            r = dc % 2
            xs = x_sb[:, dc * 1024 + t * 512: dc * 1024 + t * 512 + 512]
            tkc = emit("act", lambda a, xs=xs, r=r: a.copy(xb2[:, r * 512:(r + 1) * 512], xs),
                       deps=xch[dc][t].rdep() + bxb[r].wdep(), inc="act")
            bxb[r].w(tkc)
            tkn = emit("dve", lambda v, xs=xs, r=r: v.tensor_mul(
                xq2[:, r * 512:(r + 1) * 512], xs, xs),
                deps=xch[dc][t].rdep() + bxq[r].wdep(), inc="dve")
            bxq[r].w(tkn)
            xch[dc][t].r(tkn)
            xch[dc][t].r(tkc)
            emit("pe", lambda tt_, r=r, dc=dc: tt_.matmul(
                pbk[4][:, :], onesb[:, :], xb2[:, r * 512:(r + 1) * 512],
                start=(dc == 0), stop=(dc == NDC - 1)),
                deps=bxb[r].rdep() + bxq[r].rdep() +
                (pbuf[4].wdep() + pbuf[5].wdep() if dc == 0 else []) +
                [MEMSET_TOK, CONSTS_TOK])
            t2 = emit("pe", lambda tt_, r=r, dc=dc: tt_.matmul(
                pbk[5][:, :], onesb[:, :], xq2[:, r * 512:(r + 1) * 512],
                start=(dc == 0), stop=(dc == NDC - 1)), inc="pe")
            bxb[r].r(t2)
            bxq[r].r(t2)
            if dc == NDC - 1:
                pbuf[4].w(t2)
                pbuf[5].w(t2)
        tkn = emit("dve", lambda v: v.tensor_scalar_mul(mean[:, :], pbk[4][:, :], 1.0 / 2048.0),
                   deps=pbuf[4].rdep() + bmean.wdep(), inc="dve")
        pbuf[4].r(tkn)
        bmean.w(tkn)
        emit("dve", lambda v: v.tensor_mul(var[:, :], mean[:, :], mean[:, :]),
             deps=bmean.rdep() + bvar.wdep())
        tkn = emit("dve", lambda v: v.scalar_tensor_tensor(
            var[:, :], pbk[5][:, :], 1.0 / 2048.0, var[:, :], ALU.mult, ALU.subtract),
            deps=pbuf[5].rdep(), inc="dve")
        pbuf[5].r(tkn)
        bvar.w(tkn)
        tkn = emit("act", lambda a: a.activation(rstd[:, :], var[:, :], AF.Sqrt, bias=eps1[:, :]),
                   deps=bvar.rdep() + brstd.wdep() + [MEMSET_TOK], inc="act")
        bvar.r(tkn)
        tkn = emit("dve", lambda v: v.reciprocal(rstd[:, :], rstd[:, :]), deps=[tkn], inc="dve")
        brstd.w(tkn)

    def emit_ln_normalize(t, scr):
        lnt = scr["ln"]
        lnch = scr["lnch"]
        mean, rstd, xm = scr["mean"], scr["rstd"], scr["xm"]
        bmean, bvar, brstd, bxm = scr["bmean"], scr["bvar"], scr["brstd"], scr["bxm"]
        for dc in range(NDC):
            xs = x_sb[:, dc * 1024 + t * 512: dc * 1024 + t * 512 + 512]
            emit("dve", lambda v, xs=xs: v.tensor_sub(xm[:, :], xs, mean[:, :]),
                 deps=xch[dc][t].rdep() + bmean.rdep() + bxm.wdep() + bvar.rdep())
            tkn = emit("dve", lambda v, dc=dc, lnt=lnt: v.tensor_mul(
                lnt[:, dc * 512:(dc + 1) * 512], xm[:, :], rstd[:, :]),
                deps=lnch[dc].wdep(), inc="dve")
            bxm.w(tkn)
            xch[dc][t].r(tkn)
            bmean.r(tkn)
            brstd.r(tkn)
            lnch[dc].w(tkn)

    def emit_layernorm(t, scr):
        emit_ln_stats(t, scr)
        emit_ln_normalize(t, scr)

    def load_gu_slice(which, s, quarter):
        def fn(sy):
            w = ag_wg if which == 0 else ag_wu
            src = w.rearrange("(dc p) c -> p dc c", p=128)[:, :, s * 256:(s + 1) * 256]
            dst = wbuf[:, quarter * 4096:(quarter + 1) * 4096].rearrange(
                "p (dc c) -> p dc c", c=256)
            return sy.dma_start(out=dst, in_=src)
        return fn

    def load_down_slice(dc, base_q):
        def fn(sy):
            src = ag_wd.rearrange("(hc p) c -> p hc c", p=128)[
                :, :, dc * 128:(dc + 1) * 128]
            dst = wbuf[:, base_q * 4096:(base_q + 2) * 4096].rearrange(
                "p (hc c) -> p hc c", c=128)
            return sy.dma_start(out=dst, in_=src)
        return fn

    def load_proj_slice(kind, s, base_q):
        def fn(sy):
            w = {"q": ag_wq, "o": ag_wo, "k": ag_wk, "v": ag_wv}[kind]
            src = w.rearrange("(dc p) c -> p dc c", p=128)[
                :, :, s * 512:(s + 1) * 512]
            dst = wbuf[:, base_q * 4096:(base_q + 2) * 4096].rearrange(
                "p (dc c) -> p dc c", c=512)
            return sy.dma_start(out=dst, in_=src)
        return fn

    def emit_gate_up(t, scr, gbufs, g_sb):
        lnt = scr["ln"]
        lnch = scr["lnch"]
        for which in (0, 1):
            for s in range(32):
                q = s % 4
                tokw = emit("sp", load_gu_slice(which, s, q),
                            deps=wq4[q].wdep() + [CC_WG if which == 0 else CC_WU],
                            inc=f"sw{q}", inc_n=16)
                wq4[q].w(tokw)
                for hl in range(2):
                    hc = s * 2 + hl
                    bank = hc % 2
                    deps = wq4[q].rdep() + pbuf[bank].wdep()
                    tokm = None
                    for dc in range(NDC):
                        tokm = emit("pe", lambda tt_, dc=dc, hl=hl, q=q, lnt=lnt, bank=bank: tt_.matmul(
                            pbk[bank][:, :],
                            wbuf[:, q * 4096 + dc * 256 + hl * 128: q * 4096 + dc * 256 + (hl + 1) * 128],
                            lnt[:, dc * 512:(dc + 1) * 512],
                            start=(dc == 0), stop=(dc == NDC - 1)),
                            deps=(deps if dc == 0 else []) + lnch[dc].rdep(),
                            inc=("pe" if dc == NDC - 1 else None))
                    pbuf[bank].w(tokm)
                    wq4[q].r(tokm)
                    for dc in range(NDC):
                        lnch[dc].r(tokm)
                    if which == 0:
                        tokg = emit("act", lambda a, hc=hc, bank=bank: a.activation(
                            g_sb[:, hc * 512:(hc + 1) * 512], pbk[bank][:, :],
                            AF.Silu, bias=zb[:, :]),
                            deps=pbuf[bank].rdep() + gbufs[hc].wdep() + [MEMSET_TOK],
                            inc="act")
                    else:
                        tokg = emit("dve", lambda v, hc=hc, bank=bank: v.tensor_mul(
                            g_sb[:, hc * 512:(hc + 1) * 512],
                            g_sb[:, hc * 512:(hc + 1) * 512], pbk[bank][:, :]),
                            deps=pbuf[bank].rdep() + gbufs[hc].wdep(), inc="dve")
                    pbuf[bank].r(tokg)
                    gbufs[hc].w(tokg)

    def emit_down(t, scr, gbufs, g_sb):
        for dc in range(NDC):
            bq = (dc % 2) * 2
            tokd = emit("sp", load_down_slice(dc, bq),
                        deps=wq4[bq].wdep() + wq4[bq + 1].wdep() + [CC_WD],
                        inc=f"sw{bq}", inc_n=16)
            wq4[bq].w(tokd)
            wq4[bq + 1].w(tokd)
            bank = 2 + dc % 2
            deps = [tokd] + pbuf[bank].wdep() + \
                [tk for hcc in range(NHC) for tk in gbufs[hcc].rdep()]
            tokdm = None
            for hcc in range(NHC):
                tokdm = emit("pe", lambda tt_, hcc=hcc, bq=bq, bank=bank: tt_.matmul(
                    pbk[bank][:, :],
                    wbuf[:, bq * 4096 + hcc * 128:bq * 4096 + (hcc + 1) * 128],
                    g_sb[:, hcc * 512:(hcc + 1) * 512],
                    start=(hcc == 0), stop=(hcc == NHC - 1)),
                    deps=(deps if hcc == 0 else []),
                    inc=("pe" if hcc == NHC - 1 else None))
            pbuf[bank].w(tokdm)
            wq4[bq].r(tokdm)
            wq4[bq + 1].r(tokdm)
            for hcc in range(NHC):
                gbufs[hcc].r(tokdm)
            xs = x_sb[:, dc * 1024 + t * 512: dc * 1024 + t * 512 + 512]
            tokev = emit("dve", lambda v, xs=xs, bank=bank: v.scalar_tensor_tensor(
                xs, pbk[bank][:, :], 0.5, xs, ALU.mult, ALU.add),
                deps=pbuf[bank].rdep() + xch[dc][t].wdep(), inc="dve")
            pbuf[bank].r(tokev)
            xch[dc][t].w(tokev)
        return [tokdm, tokev]

    def mk_ln_scr(ar, bar=()):
        vxm = ar.al("vxm", [128, 512], F32)
        return {"ln": ar.al("ln", [128, 16 * 512], BF16),
                "lnch": [Buf(bar) for _ in range(NDC)],
                "xb2": ar.al("xb2", [128, 1024], BF16),
                "xq2": ar.al("xq2", [128, 1024], BF16),
                "mean": ar.al("mean", [128, 512], F32),
                "var": vxm,
                "rstd": ar.al("rstd", [128, 512], F32),
                "xm": vxm,
                "bxb": [Buf(bar), Buf(bar)], "bxq": [Buf(bar), Buf(bar)],
                "bmean": Buf(bar), "bvar": Buf(bar), "brstd": Buf(bar),
                "bxm": Buf(bar)}

    bar = snap()
    far = Arena("f1")
    PH = phases
    fscr = mk_ln_scr(far, bar)
    g_sb = far.al("g", [128, 64 * 512], BF16)
    gbufs = [Buf(bar) for _ in range(NHC)]
    if PH >= 2:
        emit_layernorm(0, fscr)
        emit_gate_up(0, fscr, gbufs, g_sb)
        emit_layernorm(1, fscr)
        emit_down(0, fscr, gbufs, g_sb)
        emit_gate_up(1, fscr, gbufs, g_sb)

    def _emit_output():
        bar2 = snap()
        oar2 = Arena("out")
        xioo2 = [oar2.al(f"xioo{i}", [128, 2048], F32) for i in range(2)]
        xioob2 = [Buf(bar2), Buf(bar2)]
        for tt in range(8):
            r = tt % 2
            tloc = tt // 4
            ev_toks = []
            for dc in range(NDC):
                tk = transp(lambda dc=dc, tt=tt: x_sb[:, dc * 1024 + tt * 128: dc * 1024 + (tt + 1) * 128],
                            xioo2[r], dc * 128, (dc + 1) * 128,
                            xch[dc][tloc].rdep() + xioob2[r].wdep(), fp32=True)
                xch[dc][tloc].r(tk)
                ev_toks.append(tk)
            xioob2[r].w(ev_toks)
            tkn = emit("sp", lambda s, tt=tt, r=r: s.dma_start(
                out=out_d[tt * 128:(tt + 1) * 128, :], in_=xioo2[r][:, :]),
                deps=xioob2[r].rdep(), inc=f"sos{r}", inc_n=16)
            xioob2[r].r(tkn)

    def _finish():
        st["sp"].append(lambda e, v=cnt["sos0"]: e.wait_ge(sems["sos0"], v))
        st["sp"].append(lambda e, v=cnt["sos1"]: e.wait_ge(sems["sos1"], v))
        with nc.Block() as block:
            block.gpsimd(lambda g: [f(g) for f in st["gp"]])
            block.sync(lambda s: [f(s) for f in st["sp"]])
            block.tensor(lambda tt_: [f(tt_) for f in st["pe"]])
            block.vector(lambda v: [f(v) for f in st["dve"]])
            block.scalar(lambda a: [f(a) for f in st["act"]])
        es.close()
        return nc

    if PH < 3:
        if PH >= 2:
            emit_down(1, fscr, gbufs, g_sb)
        _emit_output()
        return _finish()
    qar = Arena("qk")
    qscr = mk_ln_scr(qar, snap())
    emit_layernorm(0, qscr)
    d1_toks = emit_down(1, fscr, gbufs, g_sb)
    bar = snap()
    qkv_own = qar.al("qkv_own", [128, 24 * 1024], BF16)
    bt_s = qar.al("bt", [128, 2048], BF16)
    cosT = qar.al("cosT", [128, 1024], F32)
    sinT = qar.al("sinT", [128, 1024], F32)
    sqs = qar.al("sqs", [128, 512], BF16)
    kn2sb = qar.al("kn2sb", [128, 1024], BF16)
    mxr = qar.al("mxr", [128, 8], F32)
    qkvb = [Buf(bar) for _ in range(24)]
    b_bt = Buf(bar)
    b_sq = Buf(bar)
    b_kn2 = Buf(bar)
    b_mx = Buf(bar)
    b_ropetab = Buf(bar)

    t_c = emit("gp", lambda g: g.dma_start(out=cosT[:, :], in_=ropec),
               deps=b_ropetab.wdep() + d1_toks, inc="ga0", inc_n=16)
    t_s = emit("gp", lambda g: g.dma_start(out=sinT[:, :], in_=ropes), inc="ga1", inc_n=16)
    b_ropetab.w([t_c, t_s])

    kvst_toks = []

    def emit_rope(ch0, th, gi):
        c0 = 0 if th is None else th * 512
        w = 1024 if th is None else 512
        dve_toks = []
        for i in range(2):
            ch = ch0 + i
            emit("dve", lambda v, ch=ch, i=i, c0=c0, w=w: v.tensor_mul(
                bt_s[:, i * 1024 + c0:i * 1024 + c0 + w],
                qkv_own[:, ch * 1024 + c0:ch * 1024 + c0 + w], sinT[:, c0:c0 + w]),
                deps=qkvb[ch].rdep() + b_bt.wdep() + b_ropetab.rdep())
            tkn = emit("dve", lambda v, ch=ch, c0=c0, w=w: v.tensor_mul(
                qkv_own[:, ch * 1024 + c0:ch * 1024 + c0 + w],
                qkv_own[:, ch * 1024 + c0:ch * 1024 + c0 + w], cosT[:, c0:c0 + w]),
                deps=qkvb[ch].wdep(), inc="dve")
            dve_toks.append(tkn)
        b_bt.w(dve_toks)
        b_ropetab.r(dve_toks[-1])
        sw_toks = []
        for d0, s0 in ((0, 32), (32, 0), (64, 96), (96, 64)):
            tkn = emit("gp", lambda g, ch0=ch0, d0=d0, s0=s0, c0=c0, w=w: g.dma_start(
                out=qkv_own[d0:d0 + 32, ch0 * 1024:(ch0 + 2) * 1024].rearrange(
                    "p (j c) -> p j c", c=1024)[:, :, c0:c0 + w],
                in_=bt_s[s0:s0 + 32, 0:2048].rearrange(
                    "p (j c) -> p j c", c=1024)[:, :, c0:c0 + w],
                accum_op=ALU.add),
                deps=dve_toks + b_bt.rdep(), inc=f"gr{gi}", inc_n=16)
            sw_toks.append(tkn)
        b_bt.r(sw_toks[-1])
        for i in range(2):
            qkvb[ch0 + i].w([sw_toks[-1]])

    def emit_kn2(t):
        bank = 4 + t
        for kc in range(4):
            ch = 16 + kc
            tkn = emit("act", lambda a, ch=ch, t=t: a.square(
                sqs[:, :], qkv_own[:, ch * 1024 + t * 512:ch * 1024 + t * 512 + 512]),
                deps=qkvb[ch].rdep() + b_sq.wdep(), inc="act")
            b_sq.w(tkn)
            qkvb[ch].r(tkn)
            tm = emit("pe", lambda tt_, kc=kc, bank=bank: tt_.matmul(
                pbk[bank][0:8, :], indk[:, kc * 8:kc * 8 + 8], sqs[:, :],
                start=(kc == 0), stop=(kc == 3)),
                deps=b_sq.rdep() + (pbuf[bank].wdep() if kc == 0 else []) + [CONSTS_TOK],
                inc="pe")
            if kc == 3:
                pbuf[bank].w(tm)
            b_sq.r(tm)
        tkn = emit("dve", lambda v, t=t, bank=bank: v.tensor_copy(
            kn2sb[0:8, t * 512:(t + 1) * 512], pbk[bank][0:8, :]),
            deps=pbuf[bank].rdep() + b_kn2.wdep(), inc="dve")
        pbuf[bank].r(tkn)
        b_kn2.r(tkn)

    def emit_kv_stores(t, chs, is_v):
        for ch in chs:
            kc = ch - (20 if is_v else 16)
            for half in range(2):
                kvh = 2 * kc + half
                r0 = 128 * kvh + (64 if is_v else 0)
                tkn = emit("sp", lambda s, ch=ch, half=half, r0=r0, t=t: s.dma_start(
                    out=kv_in[r0:r0 + 64, t * 512:t * 512 + 512],
                    in_=qkv_own[64 * half:64 * (half + 1),
                                ch * 1024 + t * 512:ch * 1024 + t * 512 + 512]),
                    deps=qkvb[ch].rdep(), inc="skv", inc_n=16)
                qkvb[ch].r(tkn)
                kvst_toks.append(tkn)

    qst_toks = []

    def emit_q_stores(chs):
        for ch in chs:
            tkn = emit("sp", lambda s, ch=ch: s.dma_start(
                out=q_in[ch * 128:(ch + 1) * 128, :],
                in_=qkv_own[:, ch * 1024:(ch + 1) * 1024]),
                deps=qkvb[ch].rdep(), inc="sqs", inc_n=16)
            qkvb[ch].r(tkn)
            qst_toks.append(tkn)

    CCH = {}

    def emit_m2_ar_ckv():
        b_kn2.w(list(b_kn2.rd))
        for tt2 in range(2):
            bank = 4 + tt2
            tm = emit("pe", lambda tt_, tt2=tt2, bank=bank: tt_.matmul(
                pbk[bank][:, :], repmap[:, :], kn2sb[0:8, tt2 * 512:(tt2 + 1) * 512],
                start=True, stop=True),
                deps=b_kn2.rdep() + pbuf[bank].wdep() + [CONSTS_TOK], inc="pe")
            pbuf[bank].w(tm)
            b_kn2.r(tm)
        emit("dve", lambda v: v.reduce_max(mxr[:, 0:1], pbk[4][:, :], axis=AX.X),
             deps=pbuf[4].rdep() + b_mx.wdep())
        emit("dve", lambda v: v.reduce_max(mxr[:, 1:2], pbk[5][:, :], axis=AX.X),
             deps=pbuf[5].rdep())
        tkn = emit("dve", lambda v: v.tensor_max(mxr[:, 0:1], mxr[:, 0:1], mxr[:, 1:2]), inc="dve")
        pbuf[4].r(tkn)
        pbuf[5].r(tkn)
        b_mx.w(tkn)
        tkn = emit("pe", lambda tt_: tt_.transpose(
            pbk[4][0:1, 0:8], mxr[0:8, 0:1], idenf[0:8, 0:8]),
            deps=b_mx.rdep() + pbuf[4].wdep() + [CONSTS_TOK], inc="pe")
        pbuf[4].w(tkn)
        b_mx.r(tkn)
        tkn = emit("dve", lambda v: v.tensor_copy(mxr[0:1, 0:8], pbk[4][0:1, 0:8]),
                   deps=pbuf[4].rdep() + b_mx.wdep(), inc="dve")
        pbuf[4].r(tkn)
        b_mx.w(tkn)
        tok_ar_in = emit("gp", lambda g: g.dma_start(out=ar_in, in_=mxr[0:1, 0:8]),
                         deps=b_mx.rdep(), inc="gm2", inc_n=16)
        b_mx.r(tok_ar_in)
        CCH["AR"] = emit("gp", lambda g: g.collective_compute(
            "AllReduce", ALU.max, replica_groups=[list(range(NCORE))],
            ins=[ar_in], outs=[ar_out]), deps=[tok_ar_in], inc="cc")
        CCH["KV"] = emit("gp", lambda g: g.collective_compute(
            "AllGather", ALU.bypass, replica_groups=[list(range(NCORE))],
            ins=[kv_in], outs=[kv_all]), deps=list(kvst_toks), inc="cc")

    for t in range(2):
        if t == 1:
            emit_ln_normalize(1, qscr)
        jobs = [("k", 0, 16), ("v", 0, 20)] + [("q", s, 4 * s) for s in range(4)]
        for jn, (kind, s, ch0) in enumerate(jobs):
            if t == 0 and jn == 4:
                emit_ln_stats(1, qscr)
            bq = (jn % 2) * 2
            tokw = emit("sp", load_proj_slice(kind, s, bq),
                        deps=wq4[bq].wdep() + wq4[bq + 1].wdep() +
                        [{"q": CC_WQ, "o": CC_WO, "k": CC_WK, "v": CC_WV}[kind]],
                        inc=f"sw{bq}", inc_n=16)
            wq4[bq].w(tokw)
            wq4[bq + 1].w(tokw)
            for ol in range(4):
                ch = ch0 + ol
                bank = ol % 2
                deps = [tokw] + pbuf[bank].wdep()
                tokm = None
                for dc in range(NDC):
                    tokm = emit("pe", lambda tt_, dc=dc, ol=ol, bq=bq, bank=bank: tt_.matmul(
                        pbk[bank][:, :],
                        wbuf[:, bq * 4096 + dc * 512 + ol * 128: bq * 4096 + dc * 512 + (ol + 1) * 128],
                        qscr["ln"][:, dc * 512:(dc + 1) * 512],
                        start=(dc == 0), stop=(dc == NDC - 1)),
                        deps=(deps if dc == 0 else []) + qscr["lnch"][dc].rdep(),
                        inc=("pe" if dc == NDC - 1 else None))
                pbuf[bank].w(tokm)
                wq4[bq].r(tokm)
                wq4[bq + 1].r(tokm)
                for dc in range(NDC):
                    qscr["lnch"][dc].r(tokm)
                tokev = emit("dve", lambda v, ch=ch, t=t, bank=bank: v.tensor_copy(
                    qkv_own[:, ch * 1024 + t * 512: ch * 1024 + t * 512 + 512],
                    pbk[bank][:, :]),
                    deps=pbuf[bank].rdep() + qkvb[ch].wdep(), inc="dve")
                pbuf[bank].r(tokev)
                qkvb[ch].r(tokev)
            if kind == "k":
                for ch in range(16, 20):
                    qkvb[ch].w(qkvb[ch].wr + list(qkvb[ch].rd))
                emit_rope(16, t, 2 * t)
                emit_rope(18, t, 2 * t + 1)
                emit_kn2(t)
                emit_kv_stores(t, [16, 17, 18, 19], False)
            elif kind == "v":
                for ch in range(20, 24):
                    qkvb[ch].w(qkvb[ch].wr + list(qkvb[ch].rd))
                emit_kv_stores(t, [20, 21, 22, 23], True)
                if t == 1:
                    emit_m2_ar_ckv()
            elif t == 1:
                for ch in range(ch0, ch0 + 4):
                    qkvb[ch].w(qkvb[ch].wr + list(qkvb[ch].rd))
                emit_rope(ch0, None, 4 + 2 * s)
                emit_rope(ch0 + 2, None, 5 + 2 * s)
                emit_q_stores([ch0, ch0 + 1, ch0 + 2, ch0 + 3])
    CC_Q = emit("gp", lambda g: g.collective_compute(
        "AllGather", ALU.bypass, replica_groups=[list(range(NCORE))],
        ins=[q_in], outs=[q_all]), deps=qst_toks, inc="cc")

    if PH < 4:
        _emit_output()
        return _finish()
    bar = snap()
    barn = Arena("att")
    Qa2 = [barn.al(f"Qa{i}", [128, 4096], BF16) for i in range(2)]
    Ka2 = [barn.al(f"Ka{i}", [128, 4096], BF16) for i in range(2)]
    Vt = barn.al("Vt", [128, 4096], BF16)
    Va2 = [barn.al(f"Va{i}", [128, 32 * 66], BF16) for i in range(2)]
    PT = [barn.al(f"PT{i}", [128, 512], BF16) for i in range(4)]
    ctxs = barn.al("ctxs", [128, 4096], BF16)
    ctxU = barn.al("ctxU", [128, 4096], F32)
    ctxtm = barn.al("ctxtm", [128, 2048], BF16)
    masksb = barn.al("masksb", [128, 2048], BF16)
    sq4 = barn.al("sq4", [128, 1024], BF16)
    tmpb = barn.al("tmpb", [128, 512], F32)
    rden = barn.al("rden", [128, 512], F32)
    assert barn.off <= SBUF_END
    bQd = [Buf(bar), Buf(bar)]
    bK2 = [Buf(bar), Buf(bar)]
    bVt = Buf(bar)
    bVa2 = [Buf(bar), Buf(bar)]
    bPT = [Buf(bar) for _ in range(4)]
    bctxq = [Buf(bar) for _ in range(8)]
    bcsq = [Buf(bar) for _ in range(8)]
    bctxtm = Buf(bar)
    bsq4 = [Buf(bar), Buf(bar)]
    btmpb = Buf(bar)
    brden = Buf(bar)
    bm2 = Buf(bar)

    MASK_TOK = emit("gp", lambda g: g.dma_start(out=masksb[:, :], in_=mask_d),
                    inc="gmsk", inc_n=16)
    tkn = emit("gp", lambda g: g.dma_start(out=m2s[0:1, 0:8], in_=ar_out),
               deps=[CCH["AR"]] + bm2.wdep(), inc="gm2", inc_n=16)
    tkn = emit("dve", lambda v: v.reduce_max(m2g[0:1, 0:1], m2s[0:1, 0:8], axis=AX.X),
               deps=[tkn], inc="dve")
    bm2.w(tkn)
    KROW_TOKS = [emit("dve", lambda v, rb=rb: v.memset(Ka2[rb][64:65, :], 1.0),
                      deps=bK2[rb].wdep(), inc="dve") for rb in range(2)]
    VONES_TOKS = [emit("dve", lambda v, rb=rb: v.memset(Va2[rb][:, :], 1.0),
                       deps=bVa2[rb].wdep(), inc="dve") for rb in range(2)]

    GKV = emit("gp", lambda g: g.dma_start(
        out=mykv.rearrange("(j r) c -> j r c", r=128),
        in_=kv_all.rearrange("(j r) c -> j r c", r=1024)[
            :, DynSlice(g.partition_id() * 128, 128), :]),
        deps=[CCH["KV"]], inc="gg", inc_n=16)
    mykv_sh = mykv.rearrange("(j r) c -> j r c", r=128)
    myq_sh = myq.rearrange("(j r) c -> j r c", r=256)

    GQB = []
    for b in range(2):
        rb = b
        ktoks = [KROW_TOKS[rb]]
        tkn = emit("gp", lambda g, b=b, rb=rb: g.dma_start(
            out=Ka2[rb][0:64, :].rearrange("p (j c) -> p j c", c=1024),
            in_=mykv_sh[4 * b:4 * b + 4, 0:64, :].rearrange("j r c -> r j c")),
            deps=[GKV] + bK2[rb].wdep(), inc=("gka" if rb == 0 else "gcp"), inc_n=16)
        ktoks.append(tkn)
        bK2[rb].w(ktoks)
        tkn = emit("gp", lambda g, b=b: g.dma_start(
            out=Vt[0:64, :].rearrange("p (j c) -> p j c", c=1024),
            in_=mykv_sh[4 * b:4 * b + 4, 64:128, :].rearrange("j r c -> r j c")),
            deps=[GKV] + bVt.wdep(), inc="gvt", inc_n=16)
        bVt.w(tkn)
        va_toks = [VONES_TOKS[rb]]
        for kc in range(32):
            tk = transp(lambda kc=kc: Vt[0:64, kc * 128:(kc + 1) * 128],
                        Va2[rb], kc * 66, kc * 66 + 64,
                        bVt.rdep() + bVa2[rb].wdep() + [VONES_TOKS[rb]], rows=64)
            bVt.r(tk)
            va_toks.append(tk)
        bVa2[rb].w(va_toks)
    for bb in range(2):
        tkn = emit("gp", lambda g, bb=bb: g.dma_start(
            out=myq.rearrange("(j r) c -> j r c", r=256)[4 * bb:4 * bb + 4],
            in_=q_all.rearrange("(j r) c -> j r c", r=2048)[
                4 * bb:4 * bb + 4, DynSlice(g.partition_id() * 256, 256), :]),
            deps=[CC_Q], inc=f"gqa{bb}", inc_n=16)
        GQB.append(tkn)

    heads = [(b, hq) for b in range(2) for hq in range(4)]
    qload = [None, None]
    qrow = [[None] * 8, [None] * 8]
    qpend = [None, None]

    def emit_qa_load(i):
        b, hq = heads[i]
        r = i % 2
        Qh = Qa2[r]
        tokq = emit("gp", lambda g, b=b, hq=hq, Qh=Qh: g.dma_start(
            out=Qh[0:64, :].rearrange("p (j c) -> p j c", c=1024),
            in_=myq_sh[4 * b:4 * b + 4, 64 * hq:64 * hq + 64, :].rearrange(
                "j r c -> r j c")),
            deps=[GQB[b]] + bQd[r].wdep(), inc="gqa", inc_n=16)
        qload[r] = tokq
        qpend[r] = [tokq]

    def emit_bound_sq(i, chn):
        r = i % 2
        Qh = Qa2[r]
        rr = chn % 2
        tks = emit("dve", lambda v, chn=chn, rr=rr, Qh=Qh: v.tensor_mul(
            sq4[0:64, rr * 512:(rr + 1) * 512],
            Qh[0:64, chn * 512:(chn + 1) * 512],
            Qh[0:64, chn * 512:(chn + 1) * 512]),
            deps=[qload[r]] + bsq4[rr].wdep(), inc="dve")
        bsq4[rr].w(tks)

    def emit_bound_mm(i, chn):
        r = i % 2
        Qh = Qa2[r]
        rr = chn % 2
        bank = 4 + rr
        tkm = emit("pe", lambda tt_, rr=rr, bank=bank: tt_.matmul(
            pbk[bank][0:1, :], onesb[0:64, 0:1], sq4[0:64, rr * 512:(rr + 1) * 512],
            start=True, stop=True),
            deps=bsq4[rr].rdep() + pbuf[bank].wdep() + [MEMSET_TOK], inc="pe")
        pbuf[bank].w(tkm)
        bsq4[rr].r(tkm)
        tkn = emit("dve", lambda v, bank=bank: v.tensor_scalar(
            tmpb[0:1, :], pbk[bank][0:1, :], m2g[0:1, 0:1], None, ALU.mult),
            deps=pbuf[bank].rdep() + bm2.rdep() + btmpb.wdep(), inc="dve")
        pbuf[bank].r(tkn)
        bm2.r(tkn)
        tkn = emit("act", lambda a: a.activation(
            tmpb[0:1, :], tmpb[0:1, :], AF.Sqrt, bias=zb[0:1, :]),
            deps=[tkn, MEMSET_TOK], inc="act")
        btmpb.w(tkn)
        tkn = emit("dve", lambda v, chn=chn, Qh=Qh: v.tensor_scalar_mul(
            Qh[64:65, chn * 512:(chn + 1) * 512], tmpb[0:1, :], -1.01),
            deps=btmpb.rdep(), inc="dve")
        btmpb.r(tkn)
        qrow[r][chn] = tkn
        qpend[r].append(tkn)

    def emit_qa_bound_all(i):
        emit_qa_load(i)
        for chn in range(8):
            emit_bound_sq(i, chn)
            emit_bound_mm(i, chn)
        bQd[i % 2].w(qpend[i % 2])

    for b in range(2):
        rb = b
        Ka = Ka2[rb]
        Va = Va2[rb]
        bK = bK2[rb]
        bVa = bVa2[rb]

        for hq in range(4):
            i = 4 * b + hq
            r = i % 2
            Qh = Qa2[r]
            if i == 0:
                emit_qa_bound_all(0)

            tm_toks = []
            pend_bc = [None]
            pend_tp = [None]

            def mk_norm(qb):
                def do_bc():
                    tok_bc = emit("pe", lambda tt_, qb=qb: tt_.matmul(
                        pbk[3][0:64, :], onesf1[64:65, :],
                        ctxU[64:65, qb * 512:(qb + 1) * 512], start=True, stop=True),
                        deps=bctxq[qb].rdep() + pbuf[3].wdep() + [MEMSET_TOK], inc="pe")
                    pbuf[3].w(tok_bc)
                    bctxq[qb].r(tok_bc)
                    tkr = emit("dve", lambda v: v.reciprocal(rden[0:64, :], pbk[3][0:64, :]),
                               deps=pbuf[3].rdep() + brden.wdep(), inc="dve")
                    pbuf[3].r(tkr)
                    brden.w(tkr)
                    tkn = emit("dve", lambda v, qb=qb: v.tensor_mul(
                        ctxs[0:64, qb * 512:(qb + 1) * 512],
                        ctxU[0:64, qb * 512:(qb + 1) * 512], rden[0:64, :]),
                        deps=brden.rdep() + bctxq[qb].rdep() + bcsq[qb].wdep(), inc="dve")
                    brden.r(tkn)
                    bctxq[qb].r(tkn)
                    bcsq[qb].w(tkn)

                def do_tp():
                    for kc4 in range(4):
                        kc = qb * 4 + kc4
                        tk = transp(lambda kc=kc: ctxs[0:64, kc * 128:(kc + 1) * 128],
                                    ctxtm, kc * 64, (kc + 1) * 64,
                                    bcsq[qb].rdep() + bctxtm.wdep(), rows=64)
                        bcsq[qb].r(tk)
                        tm_toks.append(tk)
                return do_bc, do_tp

            def flush_pend():
                if pend_bc[0] is not None:
                    pend_bc[0]()
                    pend_bc[0] = None
                if pend_tp[0] is not None:
                    pend_tp[0]()
                    pend_tp[0] = None

            for qb in range(8):
                nkc = 4 * (qb + 1)

                def emit_pv(kc, nkc=nkc):
                    pslot = kc % 4
                    last = (kc == nkc - 1)
                    tok_pv = emit("pe", lambda tt_, kc=kc, pslot=pslot, nkc=nkc, Va=Va: tt_.matmul(
                        pbk[2][0:65, :], Va[:, kc * 66:kc * 66 + 65], PT[pslot][:, :],
                        start=(kc == 0), stop=(kc == nkc - 1)),
                        deps=bPT[pslot].rdep() + bVa.rdep() +
                        (pbuf[2].wdep() if kc == 0 else []),
                        inc=("pe" if last else None))
                    bPT[pslot].r(tok_pv)
                    bVa.r(tok_pv)

                for kc in range(nkc):
                    sbank = kc % 2
                    dj = kc - 4 * qb
                    if kc == 3 and pend_bc[0] is not None:
                        pend_bc[0]()
                        pend_bc[0] = None
                    if kc == 7 and pend_tp[0] is not None:
                        pend_tp[0]()
                        pend_tp[0] = None
                    deps = bK.rdep() + [qload[r], qrow[r][qb]] + pbuf[sbank].wdep()
                    tok_s = emit("pe", lambda tt_, kc=kc, qb=qb, sbank=sbank, Qh=Qh, Ka=Ka: tt_.matmul(
                        pbk[sbank][:, :], Ka[0:65, kc * 128:(kc + 1) * 128],
                        Qh[0:65, qb * 512:(qb + 1) * 512], start=True, stop=True),
                        deps=deps, inc="pe")
                    pbuf[sbank].w(tok_s)
                    bK.r(tok_s)
                    bQd[r].r(tok_s)
                    pslot = kc % 4
                    tok_e = emit("act", lambda a, pslot=pslot, sbank=sbank: a.activation(
                        PT[pslot][:, :], pbk[sbank][:, :], AF.Exp, bias=zb[:, :], scale=0.125),
                        deps=pbuf[sbank].rdep() + bPT[pslot].wdep() + [MEMSET_TOK], inc="act")
                    pbuf[sbank].r(tok_e)
                    if dj >= 0:
                        tok_m = emit("dve", lambda v, pslot=pslot, dj=dj: v.tensor_mul(
                            PT[pslot][:, :], PT[pslot][:, :],
                            masksb[:, dj * 512:(dj + 1) * 512]),
                            deps=[tok_e, MASK_TOK], inc="dve")
                        bPT[pslot].w([tok_e, tok_m])
                    else:
                        bPT[pslot].w(tok_e)
                    if kc >= 1:
                        emit_pv(kc - 1)
                emit_pv(nkc - 1)
                tok_ctx = ("pe", cnt["pe"])
                pbuf[2].w(tok_ctx)
                tkc = emit("dve", lambda v, qb=qb: v.tensor_copy(
                    ctxU[0:65, qb * 512:(qb + 1) * 512], pbk[2][0:65, :]),
                    deps=[tok_ctx] + bctxq[qb].wdep(), inc="dve")
                pbuf[2].r(tkc)
                bctxq[qb].w(tkc)
                flush_pend() if qb == 7 else None
                pend_bc[0], pend_tp[0] = mk_norm(qb)
                if BOUND_SPREAD:
                    if i + 1 < 8:
                        if qb == 0:
                            emit_qa_load(i + 1)
                            emit_bound_sq(i + 1, 0)
                            emit_bound_sq(i + 1, 1)
                        else:
                            emit_bound_mm(i + 1, qb - 1)
                            if qb + 1 < 8:
                                emit_bound_sq(i + 1, qb + 1)
                elif qb == 3 and i + 1 < 8:
                    emit_qa_bound_all(i + 1)
            if BOUND_SPREAD and i + 1 < 8:
                emit_bound_mm(i + 1, 7)
                bQd[(i + 1) % 2].w(qpend[(i + 1) % 2])
            flush_pend()
            bctxtm.w(tm_toks)
            tkn = emit("sp", lambda s, b=b, hq=hq: s.dma_start(
                out=ctx_in.rearrange("(bb kc p) d -> bb p kc d", bb=2, p=128)[
                    b, :, :, hq * 64:(hq + 1) * 64],
                in_=ctxtm[:, :].rearrange("p (kc d) -> p kc d", d=64)),
                deps=bctxtm.rdep(), inc="scs", inc_n=16)
            bctxtm.r(tkn)

    CC_CTX = emit("gp", lambda g: g.collective_compute(
        "AllGather", ALU.bypass, replica_groups=[list(range(NCORE))],
        ins=[ctx_in], outs=[ctx_all]), deps=[("scs", cnt["scs"])], inc="cc")

    if PH < 5:
        _emit_output()
        return _finish()
    bar = snap()
    car = Arena("wo", start=ARENA + 26624)
    ctxld = car.al("ctxld", [128, 16 * 1024], BF16)
    cstg = [car.al(f"cstg{i}", [128, 2048], BF16) for i in range(2)]
    bctxg = [[Buf(bar), Buf(bar)] for _ in range(NDC)]
    bcstg = [Buf(bar), Buf(bar)]
    GC = emit("gp", lambda g: g.dma_start(
        out=myctx.rearrange("(s r) c -> s r c", r=1024),
        in_=ctx_all.rearrange("(s r) c -> s r c", r=8192)[
            :, DynSlice(g.partition_id() * 1024, 1024), :]),
        deps=[CC_CTX], inc="gg", inc_n=16)
    for tt in range(8):
        r = tt % 2
        tkn = emit("gp", lambda g, tt=tt, r=r: g.dma_start(
            out=cstg[r][:, :].rearrange("p (s c) -> p s c", c=256),
            in_=myctx.rearrange("(s r) c -> s r c", r=1024)[
                :, tt * 128:(tt + 1) * 128, :].rearrange("s r c -> r s c")),
            deps=[GC] + bcstg[r].wdep(), inc=f"gc{r}", inc_n=16)
        bcstg[r].w(tkn)
        for dc in range(NDC):
            tk = transp(lambda dc=dc, r=r: cstg[r][:, dc * 128:(dc + 1) * 128],
                        ctxld, dc * 1024 + tt * 128, dc * 1024 + (tt + 1) * 128,
                        bcstg[r].rdep() + bctxg[dc][tt // 4].wdep())
            bcstg[r].r(tk)
            bctxg[dc][tt // 4].r(tk)
    for dc in range(NDC):
        for tl in range(2):
            bctxg[dc][tl].w(list(bctxg[dc][tl].rd))

    far2 = Arena("f2")
    fscr2 = None
    for t in range(2):
        for s in range(4):
            if t == 1 and s == 0 and fscr2 is None:
                fscr2 = mk_ln_scr(far2, snap())
                emit_ln_stats(0, fscr2)
            bq = (s % 2) * 2
            tokw = emit("sp", load_proj_slice("o", s, bq),
                        deps=wq4[bq].wdep() + wq4[bq + 1].wdep() + [CC_WO],
                        inc=f"sw{bq}", inc_n=16)
            wq4[bq].w(tokw)
            wq4[bq + 1].w(tokw)
            for ol in range(4):
                oc = 4 * s + ol
                bank = ol % 2
                deps = [tokw] + pbuf[bank].wdep()
                tokm = None
                for dc in range(NDC):
                    tokm = emit("pe", lambda tt_, dc=dc, ol=ol, bq=bq, bank=bank, t=t: tt_.matmul(
                        pbk[bank][:, :],
                        wbuf[:, bq * 4096 + dc * 512 + ol * 128: bq * 4096 + dc * 512 + (ol + 1) * 128],
                        ctxld[:, dc * 1024 + t * 512: dc * 1024 + t * 512 + 512],
                        start=(dc == 0), stop=(dc == NDC - 1)),
                        deps=(deps if dc == 0 else []) + bctxg[dc][t].rdep(),
                        inc=("pe" if dc == NDC - 1 else None))
                pbuf[bank].w(tokm)
                wq4[bq].r(tokm)
                wq4[bq + 1].r(tokm)
                for dc in range(NDC):
                    bctxg[dc][t].r(tokm)
                xs = x_sb[:, oc * 1024 + t * 512: oc * 1024 + t * 512 + 512]
                tokev = emit("dve", lambda v, xs=xs, bank=bank: v.tensor_add(
                    xs, pbk[bank][:, :], xs),
                    deps=pbuf[bank].rdep() + xch[oc][t].wdep(), inc="dve")
                pbuf[bank].r(tokev)
                xch[oc][t].w(tokev)

    bar = snap()
    g_sb2 = far2.al("g", [128, 64 * 512], BF16)
    gbufs2 = [Buf(bar) for _ in range(NHC)]
    emit_ln_normalize(0, fscr2)
    emit_gate_up(0, fscr2, gbufs2, g_sb2)
    emit_layernorm(1, fscr2)
    emit_down(0, fscr2, gbufs2, g_sb2)
    emit_gate_up(1, fscr2, gbufs2, g_sb2)
    emit_down(1, fscr2, gbufs2, g_sb2)

    _emit_output()
    return _finish()


_NC_CACHE = {}


def kernel(L0_attn_x, L0_attn_Wq, L0_attn_Wk, L0_attn_Wv, L0_attn_Wo,
           L0_ffn_w_gate, L0_ffn_w_up, L0_ffn_w_down):
    x = np.ascontiguousarray(np.asarray(L0_attn_x, dtype=np.float32).reshape(8192, 2048))
    Wq = np.ascontiguousarray(np.asarray(L0_attn_Wq, dtype=np.float32))
    Wk = np.ascontiguousarray(np.asarray(L0_attn_Wk, dtype=np.float32))
    Wv = np.ascontiguousarray(np.asarray(L0_attn_Wv, dtype=np.float32))
    Wo = np.ascontiguousarray(np.asarray(L0_attn_Wo, dtype=np.float32))
    wg = np.asarray(L0_ffn_w_gate, dtype=np.float32)
    wu = np.asarray(L0_ffn_w_up, dtype=np.float32)
    wd = np.asarray(L0_ffn_w_down, dtype=np.float32)
    cos_t, sin_t = rope_tables()

    if "nc" not in _NC_CACHE:
        _NC_CACHE["nc"] = build_nc()
    nc = _NC_CACHE["nc"]

    in_maps = []
    for c in range(NCORE):
        in_maps.append({
            "x_slice": np.ascontiguousarray(x[c * 1024:(c + 1) * 1024]),
            "wq_sh": np.ascontiguousarray(Wq[c * 256:(c + 1) * 256]),
            "wk_sh": np.ascontiguousarray(Wk[c * 256:(c + 1) * 256]),
            "wv_sh": np.ascontiguousarray(Wv[c * 256:(c + 1) * 256]),
            "wo_sh": np.ascontiguousarray(Wo[c * 256:(c + 1) * 256]),
            "wg_sh": np.ascontiguousarray(wg[:, c * 256:(c + 1) * 256]),
            "wu_sh": np.ascontiguousarray(wu[:, c * 256:(c + 1) * 256]),
            "wd_sh": np.ascontiguousarray(wd[:, c * 1024:(c + 1) * 1024]),
            "rope_cos": cos_t[c % 4],
            "rope_sin2": sin_t[c % 4],
        })
    res = run_bass_kernel_spmd(nc, in_maps, core_ids=list(range(NCORE)))
    out = np.concatenate([r["out_slice"] for r in res.results], axis=0)
    return out.reshape(2, 4096, 2048)
